# version 08 (final)
# speedup vs baseline: 1.1065x; 1.0397x over previous
import numpy as np
import ml_dtypes
import concourse.bass as bass
import concourse.mybir as mybir
from concourse.bass_utils import run_bass_kernel_spmd
from concourse.alu_op_type import AluOpType
from contextlib import ExitStack

F32 = mybir.dt.float32
BF16 = mybir.dt.bfloat16
AF = mybir.ActivationFunctionType
AX = mybir.AxisListType
ALU = AluOpType

D = 1024
T = 2048
NT = 16
KC = 8
CW = 2064
PAD = 16

BF = np.dtype(ml_dtypes.bfloat16)



class Op:
    __slots__ = ("eng", "build", "waits", "n", "seq")

    def __init__(self, eng, build, waits):
        self.eng = eng
        self.build = build
        self.waits = waits
        self.n = 0
        self.seq = 0


class Prog:

    ENGS = ("t", "v", "s", "p", "y")

    def __init__(self, name="mk"):
        self.nc = bass.Bass("TRN2", target_bir_lowering=False)
        self.es = ExitStack()
        self.ops = {e: [] for e in self.ENGS}
        self.dma_cnt = {e: 0 for e in self.ENGS}
        self.state = {}
        self.keys = {}
        self.consumed = set()
        self.inputs = {}
        self.sb = {}
        self.dram = {}
        self._final_waits = []
        self._seq = 0

    def dram_in(self, name, shape, dtype=F32):
        h = self.nc.dram_tensor(name, list(shape), dtype, kind="ExternalInput")
        self.dram[name] = h.ap()
        return self.dram[name]

    def dram_out(self, name, shape, dtype=F32):
        h = self.nc.dram_tensor(name, list(shape), dtype, kind="ExternalOutput")
        self.dram[name] = h.ap()
        return self.dram[name]

    def sbuf(self, name, shape, dtype):
        t = self.es.enter_context(self.nc.sbuf_tensor(name, list(shape), dtype))
        self.sb[name] = t
        return t

    def psum(self, name, shape, dtype=F32):
        t = self.es.enter_context(self.nc.psum_tensor(name, list(shape), dtype))
        self.sb[name] = t
        return t

    def _ent(self, name, key):
        return self.state.setdefault((name, key), {"w": None, "r": []})

    def _dep_for_read(self, name, key, waits):
        ks = self.keys.setdefault(name, set())
        if key is None:
            tgt = list(ks) if ks else [None]
        else:
            tgt = [key]
            if None in ks:
                tgt.append(None)
        for k in tgt:
            e = self.state.get((name, k))
            if e and e["w"]:
                waits.append(e["w"])

    def _note_read(self, name, key, dep):
        self.keys.setdefault(name, set()).add(key)
        self._ent(name, key)["r"].append(dep)

    def _dep_for_write(self, name, key, waits):
        ks = self.keys.setdefault(name, set())
        if key is None:
            tgt = set(ks) | {None}
        else:
            tgt = {key}
            if None in ks:
                tgt.add(None)
        for k in tgt:
            e = self.state.get((name, k))
            if e:
                if e["w"]:
                    waits.append(e["w"])
                waits.extend(e["r"])

    def _note_write(self, name, key, dep):
        ks = self.keys.setdefault(name, set())
        if key is None:
            for k in list(ks):
                self.state.pop((name, k), None)
            ks.clear()
        ks.add(key)
        self.state[(name, key)] = {"w": dep, "r": []}

    def emit(self, eng, build, reads=(), writes=()):
        waits = []
        for r in reads:
            n, k = r if isinstance(r, tuple) else (r, None)
            self._dep_for_read(n, k, waits)
        for w in writes:
            n, k = w if isinstance(w, tuple) else (w, None)
            self._dep_for_write(n, k, waits)
        idx = len(self.ops[eng])
        dep = ("e", eng, idx)
        seen, wl = set(), []
        for w in waits:
            if w[0] == "e" and w[1] == eng and w[2] == idx:
                continue
            if w not in seen:
                seen.add(w)
                wl.append(w)
        o = Op(eng, build, wl)
        self._seq += 1
        o.seq = self._seq
        self.ops[eng].append(o)
        for w in wl:
            if w[0] == "e":
                self.consumed.add(w)
        for r in reads:
            n, k = r if isinstance(r, tuple) else (r, None)
            self._note_read(n, k, dep)
        for w in writes:
            n, k = w if isinstance(w, tuple) else (w, None)
            self._note_write(n, k, dep)
        return dep

    def dma(self, q, out, in_, reads=(), writes=(), transpose=False):
        reads = list(reads)
        writes = list(writes)
        if transpose:
            writes.append(("DMAX", None))
        else:
            reads.append(("DMAX", None))
        waits = []
        for r in reads:
            n, k = r if isinstance(r, tuple) else (r, None)
            self._dep_for_read(n, k, waits)
        for w in writes:
            n, k = w if isinstance(w, tuple) else (w, None)
            self._dep_for_write(n, k, waits)
        self.dma_cnt[q] += 1
        cnt = self.dma_cnt[q]
        dep = ("d", q, cnt)
        if cnt > self.NDR:
            waits.append(("d", q, cnt - self.NDR))

        def build(E, out=out, in_=in_, transpose=transpose):
            if transpose:
                return E.dma_start_transpose(out, in_)
            return E.dma_start(out=out, in_=in_)
        idx = len(self.ops[q])
        seen, wl = set(), []
        for w in waits:
            if w not in seen:
                seen.add(w)
                wl.append(w)
        op = Op(q, build, wl)
        op.n = cnt
        self._seq += 1
        op.seq = self._seq
        self.ops[q].append(op)
        for w in wl:
            if w[0] == "e":
                self.consumed.add(w)
        for r in reads:
            n, k = r if isinstance(r, tuple) else (r, None)
            self._note_read(n, k, dep)
        for w in writes:
            n, k = w if isinstance(w, tuple) else (w, None)
            self._note_write(n, k, dep)
        return dep

    def barrier_name(self, name):
        deps = []
        for k in list(self.keys.get(name, set())):
            e = self.state.pop((name, k), None)
            if e:
                if e["w"]:
                    deps.append(e["w"])
                deps.extend(e["r"])
        self.keys[name] = {None}
        self.state[(name, None)] = {"w": None, "r": list(set(deps))}

    def final_wait(self, deps):
        for w in deps:
            if w[0] == "e":
                self.consumed.add(w)
        self._final_waits.extend(deps)

    NDR = 8

    def finalize(self):
        nc = self.nc
        esem = {e: self.es.enter_context(nc.semaphore(f"es_{e}")) for e in self.ENGS}
        dsem = {e: [self.es.enter_context(nc.semaphore(f"ds_{e}{i}")) for i in range(self.NDR)]
                for e in self.ENGS if self.dma_cnt[e] > 0}
        incflag = {}
        prefix = {}
        for e in self.ENGS:
            c = 0
            pf = []
            for i, op in enumerate(self.ops[e]):
                f = ("e", e, i) in self.consumed
                incflag[(e, i)] = f
                if f:
                    c += 1
                pf.append(c)
            prefix[e] = pf

        def sem_target(w):
            kind, e2, v = w
            if kind == "e":
                return (esem[e2], prefix[e2][v])
            else:
                slot = (v - 1) % self.NDR
                return (dsem[e2][slot], 16 * ((v - 1) // self.NDR + 1))

        def wdim(w):
            kind, e2, v = w
            if kind == "e":
                return (("e", e2), prefix[e2][v])
            slot = (v - 1) % self.NDR
            return (("d", e2, slot), (v - 1) // self.NDR + 1)

        all_ops = []
        for e in self.ENGS:
            all_ops.extend(self.ops[e])
        all_ops.sort(key=lambda o: o.seq)
        clocks = {}
        prev_on = {e: None for e in self.ENGS}
        opof = {}
        for e in self.ENGS:
            for i, op in enumerate(self.ops[e]):
                opof[("e", e, i)] = op
        dmac = {e: 0 for e in self.ENGS}
        for op in all_ops:
            if op.n:
                opof[("d", op.eng, op.n)] = op
        for op in all_ops:
            g = dict(clocks.get(id(prev_on[op.eng]), {})) if prev_on[op.eng] is not None else {}
            kept = []
            for w in op.waits:
                dim, val = wdim(w)
                if g.get(dim, 0) >= val:
                    continue
                kept.append(w)
                prod = opof.get((w[0], w[1], w[2]))
                if prod is not None:
                    pg = clocks.get(id(prod), {})
                    for d2, v2 in pg.items():
                        if g.get(d2, 0) < v2:
                            g[d2] = v2
                g[dim] = max(g.get(dim, 0), val)
            op.waits = kept
            clocks[id(op)] = g
            prev_on[op.eng] = op

        block = self.es.enter_context(nc.Block())
        for e in self.ENGS:
            ops = self.ops[e]
            if not ops and not (e == "y" and self._final_waits):
                continue
            section = getattr(block, Prog.ENGS_FULL[e])

            def make(e=e, ops=ops):
                def _(E):
                    wm = {}
                    for i, op in enumerate(ops):
                        for w in op.waits:
                            sem, val = sem_target(w)
                            key = id(sem)
                            if wm.get(key, -1) >= val:
                                continue
                            wm[key] = val
                            E.wait_ge(sem, val)
                        inst = op.build(E)
                        if hasattr(op, "n") and op.n:
                            inst.then_inc(dsem[e][(op.n - 1) % self.NDR], 16)
                        elif incflag[(e, i)]:
                            if inst is None:
                                raise RuntimeError(f"op {e}:{i} consumed but returned no instr")
                            inst.then_inc(esem[e], 1)
                    if e == "y":
                        for w in self._final_waits:
                            sem, val = sem_target(w)
                            E.wait_ge(sem, val)
                return _
            section(make())
        self.es.close()
        return nc

Prog.ENGS_FULL = {"t": "tensor", "v": "vector", "s": "scalar", "p": "gpsimd", "y": "sync"}



class Reg:
    def __init__(self, arena, role, off, words):
        self.a = arena
        self.role = role
        self.off = off
        self.words = words

    def ap(self, lo=0, ln=None, dims=None):
        t = self.a.t
        if ln is None:
            ln = self.words - lo
        base = self.off + lo
        if dims is None:
            return t[:, base:base + ln]
        ap = [[self.a.words, 128]] + dims
        return bass.AP(t[:].tensor, base, ap)


class Arena:
    def __init__(self, P, name, words, dtype):
        self.P = P
        self.name = name
        self.words = words
        self.t = P.sbuf(name, [128, words], dtype)
        self.live = {}
        self.freed = []

    def alloc(self, role, words, off=None):
        if off is None:
            taken = sorted(self.live.values())
            cur = 0
            for (o, w) in taken:
                if cur + words <= o:
                    break
                cur = max(cur, o + w)
            off = cur
        end = off + words
        if end > self.words:
            raise RuntimeError(f"arena {self.name} overflow: {role} {off}+{words}>{self.words}")
        for r, (o, w) in self.live.items():
            if not (end <= o or off >= o + w):
                raise RuntimeError(f"arena {self.name}: {role} overlaps live {r}")
        deps = []
        keep = []
        rname = self._rname(role)
        for (o, e, ds) in self.freed:
            if not (end <= o or off >= e):
                deps.extend(ds)
                if o < off or e > end:
                    keep.append((o, e, ds))
            else:
                keep.append((o, e, ds))
        self.freed = keep
        if deps:
            self.P.state[(rname, None)] = {"w": None, "r": list(set(deps))}
            self.P.keys.setdefault(rname, set()).add(None)
        self.live[role] = (off, words)
        return Reg(self, role, off, words)

    def _rname(self, role):
        return f"{self.name}.{role}"

    def free(self, role):
        off, w = self.live.pop(role)
        rname = self._rname(role)
        deps = []
        for k in list(self.P.keys.get(rname, set())):
            e = self.P.state.pop((rname, k), None)
            if e:
                if e["w"]:
                    deps.append(e["w"])
                deps.extend(e["r"])
        self.P.keys.pop(rname, None)
        self.freed.append((off, off + w, deps))


def rname(reg):
    return reg.a._rname(reg.role)



class MK:

    def __init__(self):
        self.P = Prog()
        P = self.P
        self.host = {}
        self.x = P.sbuf("x", [128, NT * D], F32)
        self.ab = Arena(P, "AB", 57472, BF16)
        self.af = Arena(P, "AF", 3072, F32)
        self.stat = P.sbuf("stat", [128, 128], F32)
        self.ones_row = P.sbuf("ones_row", [1, 128], BF16)
        self.ps = P.psum("PS", [128, 4096], F32)
        self.cst = P.sbuf("cst", [128, 4], F32)
        self._uid = 0
        P.emit("v", lambda E: E.memset(self.ones_row[:], 1.0), writes=["ones_row"])
        P.emit("v", lambda E: E.memset(self.cst[:, 0:1], 1e-5), writes=[("cst", "eps")])

    def uid(self):
        self._uid += 1
        return self._uid

    def inp(self, name, arr):
        arr = np.ascontiguousarray(arr)
        dt = {np.dtype(np.float32): F32}[arr.dtype]
        ap = self.P.dram_in(name, arr.shape, dt)
        self.host[name] = arr
        return ap

    def load_sbuf(self, name, arr, dtype=BF16, q="p"):
        ap = self.inp("c_" + name, arr)
        t = self.P.sbuf(name, [128, arr.shape[1]] if arr.ndim == 2 else list(arr.shape), dtype)
        self.P.dma(q, t[:], ap, writes=[name])
        return t

    def load_reg(self, reg, arr, q="p"):
        name = f"c_{reg.role}_{self.uid()}"
        ap = self.inp(name, arr)
        self.P.dma(q, reg.ap(0, arr.shape[1]), ap, writes=[rname(reg)])

    def mm(self, out_ap, banks, pairs, tag=None, extra_reads=()):
        P = self.P
        reads = list(extra_reads)
        writes = [("PS", b) for b in banks]

        def build(E, pairs=pairs, out_ap=out_ap):
            n = len(pairs)
            inst = None
            for i, (l, r) in enumerate(pairs):
                inst = E.matmul(out_ap, l, r, start=(i == 0), stop=(i == n - 1))
            return inst
        return P.emit("t", build, reads=reads, writes=writes)

    def mm_raw(self, build, banks, extra_reads=()):
        return self.P.emit("t", build, reads=list(extra_reads),
                           writes=[("PS", b) for b in banks])

    def ev(self, eng, build, reads=(), writes=()):
        return self.P.emit(eng, build, reads=reads, writes=writes)

    def cast_x_bf16(self, dst, eng="s"):
        for i in range(4):
            sl = slice(i * 4 * D, (i + 1) * 4 * D)
            self.ev(eng,
                    lambda E, i=i, sl=sl: (E.copy(out=dst.ap(i * 4 * D, 4 * D), in_=self.x[:, sl])
                                           if eng == "s" else
                                           E.tensor_copy(out=dst.ap(i * 4 * D, 4 * D), in_=self.x[:, sl])),
                    reads=[("x", ti) for ti in range(i * 4, i * 4 + 4)],
                    writes=[(rname(dst), ("cast", i))])

    def transpose_fm(self, dst, src, n_tok_tiles=NT, kc=KC, cw=CW, pad=PAD, src_name=None,
                     dst_key_fn=None):
        srcn = src_name or rname(src)
        for ti in range(n_tok_tiles):
            in_ap = src.ap(ti * kc * 128, kc * 128) if isinstance(src, Reg) else src[:, ti * kc * 128:(ti + 1) * kc * 128]
            out_ap = dst.ap(pad + ti * 128, None, dims=[[cw, kc], [1, 128]])
            self.P.dma("y", out_ap, in_ap,
                       reads=[(srcn, ("cast", ti // 4)) if dst_key_fn is None else (srcn, ti)],
                       writes=[(rname(dst), ("t", ti))], transpose=True)

    def zero_pads(self, dst, kc=KC, cw=CW, pad=PAD):
        self.ev("v", lambda E: E.memset(dst.ap(0, None, dims=[[cw, kc], [1, pad]]), 0.0),
                writes=[rname(dst)])


    def newton_rsqrt(self, rcol, sscol, n, rkey, sskey, tmpcol, tmpkey):
        self.ev("v", lambda E: E.tensor_tensor(out=tmpcol, in0=rcol, in1=rcol,
                op=ALU.mult), reads=[rkey], writes=[tmpkey])
        self.ev("v", lambda E: E.tensor_tensor(out=tmpcol, in0=tmpcol, in1=sscol,
                op=ALU.mult), reads=[tmpkey, sskey], writes=[tmpkey])
        self.ev("v", lambda E: E.tensor_scalar(out=tmpcol, in0=tmpcol,
                scalar1=-0.5, scalar2=1.5, op0=ALU.mult, op1=ALU.add),
                reads=[tmpkey], writes=[tmpkey])
        self.ev("v", lambda E: E.tensor_tensor(out=rcol, in0=rcol, in1=tmpcol,
                op=ALU.mult), reads=[rkey, tmpkey], writes=[rkey])

    def ln_x_inplace(self, tiles=None, eps=1e-5, add_hbm=None):
        P = self.P
        st = self.stat
        tiles = list(tiles if tiles is not None else range(NT))
        n = len(tiles)
        SM, SS, MSQ, LV, RSTD = 0, 32, 64, 80, 96
        if add_hbm is not None:
            ring = [self.af.alloc(f"lnadd{i}", D) for i in range(2)]
            for j, ti in enumerate(tiles):
                rg = ring[j % 2]
                self.P.dma("p", rg.ap(0, D),
                           add_hbm.rearrange("(t p) d -> p t d", p=128)[:, ti, :],
                           writes=[(rname(rg), None)])
                self.ev("v", lambda E, ti=ti, rg=rg: E.tensor_tensor(
                    out=self.x[:, ti * D:(ti + 1) * D], in0=self.x[:, ti * D:(ti + 1) * D],
                    in1=rg.ap(0, D), op=ALU.add),
                    reads=[("x", ti), (rname(rg), None)], writes=[("x", ti)])
            for rg in ring:
                self.af.free(rg.role)
        lsq = self.af.alloc("lnsq", D)
        for ti in tiles:
            xs = self.x[:, ti * D:(ti + 1) * D]
            self.ev("s", lambda E, xs=xs, ti=ti: E.activation(
                out=lsq.ap(0, D), in_=xs, func=AF.Square,
                accum_out=st[:, SS + ti:SS + ti + 1]),
                reads=[("x", ti)], writes=[("stat", ("ss", ti)), rname(lsq)])
            self.ev("v", lambda E, xs=xs, ti=ti: E.tensor_reduce(
                out=st[:, SM + ti:SM + ti + 1], in_=xs, axis=AX.X, op=ALU.add),
                reads=[("x", ti)], writes=[("stat", ("sm", ti))])
        rs = [("stat", ("sm", ti)) for ti in tiles]
        rss = [("stat", ("ss", ti)) for ti in tiles]
        self.ev("v", lambda E: E.tensor_scalar(out=st[:, SM:SM + n], in0=st[:, SM:SM + n],
                scalar1=1.0 / D, scalar2=None, op0=ALU.mult), reads=rs, writes=rs)
        self.ev("v", lambda E: E.tensor_tensor(out=st[:, MSQ:MSQ + n], in0=st[:, SM:SM + n],
                in1=st[:, SM:SM + n], op=ALU.mult), reads=rs, writes=[("stat", "msq")])
        self.ev("v", lambda E: E.tensor_scalar(out=st[:, SS:SS + n], in0=st[:, SS:SS + n],
                scalar1=1.0 / D, scalar2=None, op0=ALU.mult), reads=rss, writes=rss)
        self.ev("v", lambda E: E.tensor_tensor(out=st[:, SS:SS + n], in0=st[:, SS:SS + n],
                in1=st[:, MSQ:MSQ + n], op=ALU.subtract),
                reads=[("stat", "msq")] + rss, writes=rss)
        self.ev("s", lambda E: E.activation(out=st[:, LV:LV + n], in_=st[:, SS:SS + n],
                func=AF.Ln, bias=self.cst[:, 0:1]), reads=rss + [("cst", "eps")],
                writes=[("stat", "lv")])
        self.ev("s", lambda E: E.activation(out=st[:, RSTD:RSTD + n], in_=st[:, LV:LV + n],
                func=AF.Exp, scale=-0.5), reads=[("stat", "lv")],
                writes=[("stat", ("rstd", ti)) for ti in tiles])
        for i, ti in enumerate(tiles):
            xs = self.x[:, ti * D:(ti + 1) * D]
            self.ev("v", lambda E, xs=xs, i=i: E.tensor_scalar(
                out=xs, in0=xs, scalar1=st[:, SM + i:SM + i + 1],
                scalar2=st[:, RSTD + i:RSTD + i + 1],
                op0=ALU.subtract, op1=ALU.mult),
                reads=[("x", ti), ("stat", ("sm", ti)), ("stat", ("rstd", ti))],
                writes=[("x", ti)])
        self.af.free("lnsq")

    def load_x(self, ap):
        self.P.dma("p", self.x[:].rearrange("p (t d) -> p t d", t=NT),
                   ap.rearrange("(t p) d -> p t d", p=128),
                   writes=[("x", ti) for ti in range(NT)])

    def save_x(self, ap):
        for h in range(2):
            self.P.dma("p",
                       ap.rearrange("(t p) d -> p t d", p=128)[:, h * 8:(h + 1) * 8, :],
                       self.x[:].rearrange("p (t d) -> p t d", t=NT)[:, h * 8:(h + 1) * 8, :],
                       reads=[("x", ti) for ti in range(h * 8, h * 8 + 8)])

    def store_x(self, ap):
        for h in range(2):
            dep = self.P.dma("y",
                             ap.rearrange("(t p) d -> p t d", p=128)[:, h * 8:(h + 1) * 8, :],
                             self.x[:].rearrange("p (t d) -> p t d", t=NT)[:, h * 8:(h + 1) * 8, :],
                             reads=[("x", ti) for ti in range(h * 8, h * 8 + 8)])
            self.P.final_wait([dep])

    def l0_attn(self, R_arr):
        P = self.P
        P.barrier_name("stat")
        ab = self.ab
        xb = ab.alloc("xb", NT * D)
        xT = ab.alloc("xT", KC * CW)
        eT = ab.alloc("eT", KC * 2048)
        Rr = ab.alloc("R", KC * D)
        self.load_reg(Rr, R_arr.reshape(KC, 128, D).transpose(1, 0, 2).reshape(128, KC * D))
        onesc = self.load_sbuf("onescol", np.ones((128, 1), np.float32))
        self.cast_x_bf16(xb)
        self.zero_pads(xT)
        self.transpose_fm(xT, xb)
        for m in range(KC):
            for nb in range(4):
                bank = (m * 4 + nb) % 8
                pairs = [(Rr.ap(k * D + m * 128, 128), xT.ap(k * CW + PAD + nb * 512, 512))
                         for k in range(KC)]
                self.mm(self.ps[:, bank * 512:(bank + 1) * 512], [bank], pairs,
                        extra_reads=[rname(Rr)] + [(rname(xT), ("t", t)) for t in range(nb * 4, nb * 4 + 4)])
                eng = "v" if (m * 4 + nb) % 2 == 0 else "s"
                dst = eT.ap(m * 2048 + nb * 512, 512)
                self.ev(eng, lambda E, dst=dst, bank=bank, eng=eng: (
                    E.tensor_copy(out=dst, in_=self.ps[:, bank * 512:(bank + 1) * 512])
                    if eng == "v" else
                    E.copy(out=dst, in_=self.ps[:, bank * 512:(bank + 1) * 512])),
                    reads=[("PS", bank)], writes=[(rname(eT), ("e", m, nb))])
        ab.free("R")
        A = ab.alloc("A", 2048)
        AT = ab.alloc("AT", 2048)
        bmr = ab.alloc("bmask", 2048)
        bm = np.zeros((128, 4, 512), np.float32)
        for v in range(4):
            for i in range(128):
                bm[i, v, :v * 128 + i + 1] = 1.0
        self.load_reg(bmr, bm.reshape(128, 2048))
        st = self.stat
        NMX, RIV = 112, 120
        for qi in range(NT):
            nkb = qi // 4 + 1
            nkc = nkb * 4
            vq = qi % 4
            for kb in range(nkb):
                pairs = [(eT.ap(k * 2048 + qi * 128, 128), xT.ap(k * CW + PAD + kb * 512, 512))
                         for k in range(KC)]
                self.mm(self.ps[:, kb * 512:(kb + 1) * 512], [kb], pairs,
                        extra_reads=[(rname(eT), ("e", k, qi // 4)) for k in range(KC)]
                                    + [(rname(xT), ("t", t)) for t in range(kb * 4, kb * 4 + 4)])
            sbanks = [("PS", b) for b in range(nkb)]
            self.ev("v", lambda E, nkb=nkb, qi=qi: E.reduce_max(
                out=st[:, NMX + qi % 8:NMX + qi % 8 + 1], in_=self.ps[:, 0:nkb * 512],
                axis=AX.X, negate=True),
                reads=sbanks, writes=[("stat", ("nmx", qi % 8))])
            self.ev("s", lambda E, nkb=nkb, qi=qi: E.activation(
                out=A.ap(0, nkb * 512), in_=self.ps[:, 0:nkb * 512], func=AF.Exp,
                bias=st[:, NMX + qi % 8:NMX + qi % 8 + 1]),
                reads=sbanks + [("stat", ("nmx", qi % 8))], writes=[(rname(A), None)])
            self.ev("v", lambda E, nkb=nkb, vq=vq: E.tensor_tensor(
                out=A.ap((nkb - 1) * 512, 512), in0=A.ap((nkb - 1) * 512, 512),
                in1=bmr.ap(vq * 512, 512), op=ALU.mult),
                reads=[(rname(A), None), rname(bmr)], writes=[(rname(A), None)])
            self.P.dma("y", AT.ap(0, None, dims=[[128, nkc], [1, 128]]), A.ap(0, nkc * 128),
                       reads=[(rname(A), None)], writes=[(rname(AT), None)], transpose=True)
            pairs = [(AT.ap(kc * 128, 128), onesc[:, 0:1]) for kc in range(nkc)]
            self.mm(self.ps[:, 6 * 512:6 * 512 + 1], [6], pairs,
                    extra_reads=[(rname(AT), None), "onescol"])
            self.ev("v", lambda E, qi=qi: E.reciprocal(
                out=st[:, RIV + qi % 8:RIV + qi % 8 + 1], in_=self.ps[:, 6 * 512:6 * 512 + 1]),
                reads=[("PS", 6)], writes=[("stat", ("rinv", qi % 8))])
            for n2 in range(2):
                pairs = [(AT.ap(kc * 128, 128), xb.ap(kc * D + n2 * 512, 512))
                         for kc in range(nkc)]
                self.mm(self.ps[:, (4 + n2) * 512:(5 + n2) * 512], [4 + n2], pairs,
                        extra_reads=[(rname(AT), None)] + [(rname(xb), ("cast", g)) for g in range(4)])
            self.ev("v", lambda E, qi=qi: E.scalar_tensor_tensor(
                out=self.x[:, qi * D:(qi + 1) * D], in0=self.ps[:, 4 * 512:6 * 512],
                scalar=st[:, RIV + qi % 8:RIV + qi % 8 + 1],
                in1=self.x[:, qi * D:(qi + 1) * D], op0=ALU.mult, op1=ALU.add),
                reads=[("PS", 4), ("PS", 5), ("stat", ("rinv", qi % 8)), ("x", qi)],
                writes=[("x", qi)])
        ab.free("xb")
        ab.free("xT")
        ab.free("eT")
        ab.free("A")
        ab.free("AT")
        ab.free("bmask")
        self.ln_x_inplace()



def build_program(parts, weights):
    mk = MK()
    x_in = mk.P.dram_in("x_in", [T, D], F32)
    out = mk.P.dram_out("out", [T, D], F32)
    mk.load_x(x_in)
    nsv = [0]

    def mk_save():
        nsv[0] += 1
        xs = mk.P.nc.dram_tensor(f"xsv{nsv[0]}", [T, D], F32).ap()
        mk.save_x(xs)
        return xs
    for p in parts:
        if p == "l0_attn":
            mk.l0_attn(weights["L0_attn_R"])
        elif p == "l0_ffn":
            mk.l0_ffn(weights, mk_save())
        elif p == "l0_ffn1":
            mk.l0_ffn(weights, mk_save(), depth=1, skip_final_ln=True)
        elif p.startswith("l0_ffn@"):
            sa = p.split("@")[1]
            mk.l0_ffn(weights, mk_save(), depth=1, skip_final_ln=True, stop_after=sa)
        elif p == "l1_attn":
            mk.l1_attn(weights, mk_save())
        elif p == "l1_ffn":
            mk.l1_ffn(weights)
        else:
            raise ValueError(p)
    mk.store_x(out)
    nc = mk.P.finalize()
    return nc, mk.host


def run_parts(parts, weights, x_full, cores=8):
    nc, host = build_program(parts, weights)
    in_maps = []
    for c in range(cores):
        m = dict(host)
        m["x_in"] = np.ascontiguousarray(x_full[c])
        in_maps.append(m)
    res = run_bass_kernel_spmd(nc, in_maps, core_ids=list(range(cores)))
    return np.stack([r["out"] for r in res.results], axis=0)


def kernel(**inputs):
    w = {k: np.asarray(v) for k, v in inputs.items()}
    x = w["L0_attn_encoding"]
    out = run_parts(["l0_attn", "l0_ffn", "l1_attn", "l1_ffn"], w, x, cores=8)
    return out.astype(np.float32)


def _img_kpn(w, kc=KC):
    K2, N = w.shape
    return np.ascontiguousarray(w.reshape(kc, 128, N).transpose(1, 0, 2).reshape(128, kc * N))


def _rotary_tables():
    DH, W, n = 64, 256, T
    half = np.arange(0, DH, 2, dtype=np.float64)
    inv_freq = 1.0 / 10000.0 ** (half / DH)
    t = np.arange(n, dtype=np.float64)
    freqs = t[:, None] * inv_freq[None, :]
    freqs = np.concatenate([freqs, freqs], -1)
    base_scale = (half + 0.4 * DH) / (1.4 * DH)
    power = (t - n // 2) / (W // 2)
    scale = base_scale[None, :] ** power[:, None]
    scale = np.concatenate([scale, scale], -1)
    sign = np.where(np.arange(DH) < 32, -1.0, 1.0)[None, :]
    QK_SCALE = 8.0
    cs_q = QK_SCALE * np.cos(freqs) * scale
    ss_q = QK_SCALE * np.sin(freqs) * scale * sign
    cs_k = np.cos(freqs) / scale
    ss_k = np.sin(freqs) / scale * sign
    def tok(a):
        return np.ascontiguousarray(
            a.reshape(NT, 128, DH).transpose(1, 0, 2).reshape(128, NT * DH).astype(np.float32))
    return tok(cs_q), tok(ss_q), tok(cs_k), tok(ss_k)


def _local_masks():
    m = np.zeros((3, 128, 384), np.float32)
    for v, lo in ((0, 256), (1, 128), (2, 0)):
        for i in range(128):
            a = max(i, lo)
            m[v, i, a:i + 257] = 1.0
    return m


class L0FFNMixin:
    def l0_ffn(self, w, xin_ap, depth=6, skip_final_ln=False, stop_after=None):
        P, ab, af = self.P, self.ab, self.af
        P.barrier_name("stat")
        P.barrier_name("stat2")
        st2 = self.stat2 = self.P.sb.get("stat2") or P.sbuf("stat2", [128, 288], F32)
        DEPTH, INNER, FH, NP = depth, 3072, 2816, 22
        ABW = ab.words
        AFW = af.words
        S2W = 288

        def s2ap(col, n=1, bcast=None):
            if bcast is None:
                return st2[:, col:col + n]
            return bass.AP(st2[:].tensor, col, [[S2W, 128], [1, n], [0, bcast]])

        csq, ssq, csk, ssk = _rotary_tables()
        rot = {}
        for nm, arr in (("csq", csq), ("ssq", ssq), ("csk", csk), ("ssk", ssk)):
            rot[nm] = self.load_sbuf("rot_" + nm, arr, dtype=BF16)
        lm = _local_masks()
        lmask = self.load_sbuf("lmask",
                               np.ascontiguousarray(lm.transpose(1, 0, 2).reshape(128, 3 * 384)))
        if "onescol" not in self.P.sb:
            self.load_sbuf("onescol", np.ones((128, 1), np.float32))
        onesc = self.P.sb["onescol"]
        idf = self.load_sbuf("idf", np.eye(128, dtype=np.float32), dtype=F32)
        onesrf = P.sbuf("onesrf", [1, 128], F32)
        self.ev("v", lambda E: E.memset(onesrf[:], 1.0), writes=["onesrf"])
        self.ev("v", lambda E: E.memset(self.cst[:, 1:2], 1e-12), writes=[("cst", "e12")])
        smat = np.zeros((128, 128), np.float32)
        for s in range(127):
            smat[s, s + 1] = 1.0
        emat = np.zeros((128, 128), np.float32)
        emat[127, 0] = 1.0
        smats = self.load_sbuf("smat", smat, dtype=F32)
        emats = self.load_sbuf("emat", emat, dtype=F32)
        b1c = P.sbuf("b1c", [128, 2 * NP], F32)
        lngc = P.sbuf("lngc", [128, KC], F32)
        lnbc = P.sbuf("lnbc", [128, KC], F32)
        qsb = P.sbuf("qsb", [128, 128], F32)
        b2row = P.sb.get("b2row") or P.sbuf("b2row", [1, D], BF16)
        qsrow = P.sbuf("qsrow", [1, 128], BF16)
        P.sbuf("mrow2", [1, 128], F32)
        aos = self.dram_scratch_bf16()
        W1pads, W2pads = [], []
        for i in range(DEPTH):
            W1p = np.zeros((D, 2 * FH), np.float32)
            W1p[:, :2730] = w["L0_ffn_W1"][i][:, :2730]
            W1p[:, FH:FH + 2730] = w["L0_ffn_W1"][i][:, 2730:]
            W1pads.append(W1p)
            W2p = np.zeros((FH, D), np.float32)
            W2p[:2730] = w["L0_ffn_W2"][i]
            W2pads.append(W2p)

        for li in range(DEPTH):
            xb = ab.alloc("xb", NT * D)
            self.cast_x_bf16(xb, eng="s")
            xT = ab.alloc("xT", KC * CW)
            self.zero_pads(xT)
            self.transpose_fm(xT, xb)
            ab.free("xb")
            wqkv = ab.alloc("wqkv", KC * INNER)
            self.load_reg(wqkv, _img_kpn(np.asarray(w["L0_ffn_Wqkv"][li], np.float32)))
            b1pad = np.zeros(2 * FH, np.float32)
            b1pad[:2730] = w["L0_ffn_b1"][li][:2730]
            b1pad[FH:FH + 2730] = w["L0_ffn_b1"][li][2730:]
            P.dma("p", b1c[:], self.inp(f"c_b1_{li}",
                  np.ascontiguousarray(b1pad.reshape(2 * NP, 128).T)), writes=["b1c"])
            P.dma("p", lngc[:], self.inp(f"c_lng_{li}", np.ascontiguousarray(
                  np.asarray(w["L0_ffn_ln_g"][li], np.float32).reshape(KC, 128).T)), writes=["lngc"])
            P.dma("p", lnbc[:], self.inp(f"c_lnb_{li}", np.ascontiguousarray(
                  np.asarray(w["L0_ffn_ln_b"][li], np.float32).reshape(KC, 128).T)), writes=["lnbc"])
            P.dma("p", b2row[:], self.inp(f"c_b2_{li}",
                  np.asarray(w["L0_ffn_b2"][li], np.float32)[None, :]), writes=["b2row"])
            qk = np.zeros((1, 128), np.float32)
            qk[0, :64] = w["L0_ffn_q_scale"][li]
            qk[0, 64:] = w["L0_ffn_k_scale"][li]
            P.dma("p", qsrow[:], self.inp(f"c_qk_{li}", qk), writes=["qsrow"])
            self.mm(self.ps[:, 7 * 512:7 * 512 + 128], [7],
                    [(self.ones_row[0:1, :], qsrow[0:1, :])], extra_reads=["ones_row", "qsrow"])
            self.ev("v", lambda E: E.tensor_copy(out=qsb[:], in_=self.ps[:, 7 * 512:7 * 512 + 128]),
                    reads=[("PS", 7)], writes=["qsb"])

            def shifted_col(k, c):
                return k * CW + PAD + (-1 if k >= 4 else 0) + c

            kTr = ab.alloc("kTr", 3 * 1024)
            vr = ab.alloc("vr", 3 * 1024)
            qp = ab.alloc("qp", 1024)
            qT = ab.alloc("qT", 1024)
            Ah = ab.alloc("Ah", 3072)
            ATh = ab.alloc("ATh", 3072)
            aot = ab.alloc("aot", 1024)
            qc = af.alloc("qc", 3072)
            for s in range(3):
                self.ev("v", lambda E, s=s: E.memset(kTr.ap(s * 1024, 1024), 0.0),
                        writes=[(rname(kTr), s)])
                self.ev("v", lambda E, s=s: E.memset(vr.ap(s * 1024, 1024), 0.0),
                        writes=[(rname(vr), s)])

            for ti in range(NT if stop_after != "cast" else 0):
                xt_reads = [(rname(xT), ("t", tt)) for tt in {ti, max(ti - 1, 0)}]
                for nb in range(6):
                    pairs = [(xT.ap(shifted_col(k, ti * 128), 128),
                              wqkv.ap(k * INNER + nb * 512, 512)) for k in range(KC)]
                    self.mm(self.ps[:, nb * 512:(nb + 1) * 512], [nb], pairs,
                            extra_reads=xt_reads + [rname(wqkv)])
                self.ev("v", lambda E, ti=ti: E.tensor_copy(
                    out=vr.ap((ti % 3) * 1024, 1024), in_=self.ps[:, 4 * 512:6 * 512]),
                    reads=[("PS", 4), ("PS", 5)], writes=[(rname(vr), ti % 3)])
                if stop_after == "qkv":
                    continue
                for qki, (bank0, tabc, tabs, qso, hscol, rscol) in enumerate(
                        ((0, "csq", "ssq", 0, 0, 32), (2, "csk", "ssk", 64, 16, 48))):
                    psrc = self.ps[:, bank0 * 512:(bank0 + 2) * 512]
                    pb = [("PS", bank0), ("PS", bank0 + 1)]
                    self.ev("s", lambda E, psrc=psrc: E.activation(
                        out=af.t[:, qc.off:qc.off + 1024], in_=psrc, func=AF.Square),
                        reads=pb, writes=[(rname(qc), "A")])
                    self.ev("v", lambda E, hscol=hscol: E.tensor_reduce(
                        out=s2ap(hscol, 16),
                        in_=bass.AP(af.t[:].tensor, qc.off, [[AFW, 128], [64, 16], [1, 64]]),
                        axis=AX.X, op=ALU.add),
                        reads=[(rname(qc), "A")], writes=[("stat2", ("hs", qki))])
                    self.ev("s", lambda E, hscol=hscol, rscol=rscol: E.activation(
                        out=s2ap(rscol, 16), in_=s2ap(hscol, 16), func=AF.Ln,
                        bias=self.cst[:, 1:2]),
                        reads=[("stat2", ("hs", qki)), ("cst", "e12")],
                        writes=[("stat2", ("lnh", qki))])
                    self.ev("s", lambda E, rscol=rscol: E.activation(
                        out=s2ap(rscol, 16), in_=s2ap(rscol, 16), func=AF.Exp, scale=-0.5),
                        reads=[("stat2", ("lnh", qki))], writes=[("stat2", ("rstd", qki))])
                    self.newton_rsqrt(s2ap(rscol, 16), s2ap(hscol, 16), 16,
                                      ("stat2", ("rstd", qki)), ("stat2", ("hs", qki)),
                                      s2ap(256, 16), ("stat2", "nsc"))
                    self.ev("v", lambda E, psrc=psrc, rscol=rscol: E.tensor_tensor(
                        out=bass.AP(af.t[:].tensor, qc.off + 1024, [[AFW, 128], [64, 16], [1, 64]]),
                        in0=bass.AP(self.ps[:].tensor, psrc.offset, [[4096, 128], [64, 16], [1, 64]]),
                        in1=s2ap(rscol, 16, bcast=64), op=ALU.mult),
                        reads=pb + [("stat2", ("rstd", qki))], writes=[(rname(qc), "B")])
                    self.ev("p", lambda E, qso=qso: E.tensor_tensor(
                        out=bass.AP(af.t[:].tensor, qc.off, [[AFW, 128], [64, 16], [1, 64]]),
                        in0=bass.AP(af.t[:].tensor, qc.off + 1024, [[AFW, 128], [64, 16], [1, 64]]),
                        in1=bass.AP(qsb[:].tensor, qso, [[128, 128], [0, 16], [1, 64]]),
                        op=ALU.mult),
                        reads=[(rname(qc), "B"), "qsb"], writes=[(rname(qc), "A")])
                    self.ev("p", lambda E, tabc=tabc, ti=ti: E.tensor_tensor(
                        out=bass.AP(af.t[:].tensor, qc.off + 1024, [[AFW, 128], [64, 16], [1, 64]]),
                        in0=bass.AP(af.t[:].tensor, qc.off, [[AFW, 128], [64, 16], [1, 64]]),
                        in1=bass.AP(rot[tabc][:].tensor, ti * 64, [[NT * 64, 128], [0, 16], [1, 64]]),
                        op=ALU.mult),
                        reads=[(rname(qc), "A"), "rot_" + tabc], writes=[(rname(qc), "B")])
                    self.ev("p", lambda E, tabs=tabs, ti=ti: E.tensor_tensor(
                        out=bass.AP(af.t[:].tensor, qc.off + 2048, [[AFW, 128], [64, 16], [1, 64]]),
                        in0=bass.AP(af.t[:].tensor, qc.off + 32, [[AFW, 128], [64, 16], [-32, 2], [1, 32]]),
                        in1=bass.AP(rot[tabs][:].tensor, ti * 64, [[NT * 64, 128], [0, 16], [1, 64]]),
                        op=ALU.mult),
                        reads=[(rname(qc), "A"), "rot_" + tabs], writes=[(rname(qc), "C")])
                    self.ev("v", lambda E: E.tensor_tensor(
                        out=qp.ap(0, 1024), in0=af.t[:, qc.off + 1024:qc.off + 2048],
                        in1=af.t[:, qc.off + 2048:qc.off + 3072], op=ALU.add),
                        reads=[(rname(qc), "B"), (rname(qc), "C")], writes=[rname(qp)])
                    if qki == 0:
                        P.dma("y", qT.ap(0, None, dims=[[128, 8], [1, 128]]), qp.ap(0, 1024),
                              reads=[rname(qp)], writes=[rname(qT)], transpose=True)
                    else:
                        P.dma("y", kTr.ap((ti % 3) * 1024, None, dims=[[128, 8], [1, 128]]),
                              qp.ap(0, 1024), reads=[rname(qp)],
                              writes=[(rname(kTr), ti % 3)], transpose=True)
                if stop_after == "chains":
                    continue
                _steps = stop_after.split(":")[1] if stop_after and ":" in stop_after else "zzz"
                mv = min(ti, 2)
                for hh in range(2):
                    ktr_reads = [(rname(kTr), s) for s in range(3)]
                    vr_reads = [(rname(vr), s) for s in range(3)]

                    def sc_build(E, hh=hh, ti=ti):
                        inst = None
                        for h8 in range(8):
                            h = hh * 8 + h8
                            p0 = (h % 2) * 64
                            cb = (h8 % 2) * 1536 + (h8 // 2) * 384
                            for j, kt in enumerate((ti - 2, ti - 1, ti)):
                                sl = (kt % 3) * 1024
                                lhsT = ab.t[p0:p0 + 64, qT.off + (h // 2) * 128:qT.off + (h // 2) * 128 + 128]
                                rhs = ab.t[p0:p0 + 64, kTr.off + sl + (h // 2) * 128:kTr.off + sl + (h // 2) * 128 + 128]
                                inst = E.matmul(self.ps[:, cb + j * 128:cb + (j + 1) * 128],
                                                lhsT, rhs, start=True, stop=True)
                        return inst
                    self.mm_raw(sc_build, banks=list(range(6)),
                                extra_reads=[rname(qT)] + ktr_reads)
                    if _steps == "sc":
                        continue
                    self.ev("v", lambda E, hh=hh: E.reduce_max(
                        out=s2ap(224 + hh), in_=self.ps[:, 0:3072], axis=AX.X, negate=True),
                        reads=[("PS", b) for b in range(6)], writes=[("stat2", ("nmx", hh))])
                    if _steps == "max":
                        continue
                    self.ev("s", lambda E, hh=hh: E.activation(
                        out=Ah.ap(0, 3072), in_=self.ps[:, 0:3072], func=AF.Exp,
                        bias=s2ap(224 + hh)),
                        reads=[("PS", b) for b in range(6)] + [("stat2", ("nmx", hh))],
                        writes=[rname(Ah)])
                    if _steps == "exp":
                        continue
                    self.ev("v", lambda E, mv=mv: E.tensor_tensor(
                        out=Ah.ap(0, None, dims=[[384, 8], [1, 384]]),
                        in0=Ah.ap(0, None, dims=[[384, 8], [1, 384]]),
                        in1=bass.AP(lmask[:].tensor, mv * 384, [[3 * 384, 128], [0, 8], [1, 384]]),
                        op=ALU.mult),
                        reads=[rname(Ah), "lmask"], writes=[rname(Ah)])
                    if _steps == "mask":
                        continue
                    P.dma("y", ATh.ap(0, None, dims=[[128, 24], [1, 128]]), Ah.ap(0, 3072),
                          reads=[rname(Ah)], writes=[rname(ATh)], transpose=True)

                    if _steps == "at":
                        continue
                    def rs_build(E):
                        inst = None
                        for h8 in range(8):
                            pos = (h8 % 2) * 4 + h8 // 2
                            for j in range(3):
                                inst = E.matmul(self.ps[:, h8:h8 + 1],
                                                ab.t[:, ATh.off + (pos * 3 + j) * 128:ATh.off + (pos * 3 + j + 1) * 128],
                                                onesc[:, 0:1], start=(j == 0), stop=(j == 2))
                        return inst
                    self.mm_raw(rs_build, banks=[0], extra_reads=[rname(ATh), "onescol"])
                    self.ev("v", lambda E, hh=hh: E.reciprocal(
                        out=s2ap(228 + hh * 8, 8), in_=self.ps[:, 0:8]),
                        reads=[("PS", 0)], writes=[("stat2", ("rinv", hh))])

                    if _steps == "rs":
                        continue
                    def av_build(E, hh=hh, ti=ti):
                        inst = None
                        for h8 in range(8):
                            h = hh * 8 + h8
                            pos = (h8 % 2) * 4 + h8 // 2
                            for j, kt in enumerate((ti - 2, ti - 1, ti)):
                                sl = (kt % 3) * 1024
                                inst = E.matmul(self.ps[:, 3072 + h * 64:3072 + (h + 1) * 64],
                                                ab.t[:, ATh.off + (pos * 3 + j) * 128:ATh.off + (pos * 3 + j + 1) * 128],
                                                ab.t[:, vr.off + sl + h * 64:vr.off + sl + (h + 1) * 64],
                                                start=(j == 0), stop=(j == 2))
                        return inst
                    self.mm_raw(av_build, banks=[6, 7] if hh else [6],
                                extra_reads=[rname(ATh)] + vr_reads)
                if _steps == "av":
                    continue
                self.ev("v", lambda E: E.tensor_tensor(
                    out=aot.ap(0, None, dims=[[64, 16], [1, 64]]),
                    in0=bass.AP(self.ps[:].tensor, 3072, [[4096, 128], [64, 16], [1, 64]]),
                    in1=s2ap(228, 16, bcast=64), op=ALU.mult),
                    reads=[("PS", 6), ("PS", 7), ("stat2", ("rinv", 0)), ("stat2", ("rinv", 1))],
                    writes=[rname(aot)])
                P.dma("y", aos.rearrange("(t p) d -> p t d", p=128)[:, ti, :], aot.ap(0, 1024),
                      reads=[rname(aot)], writes=[("aos", ti)])
            for r in ("wqkv", "kTr", "vr", "qp", "qT", "Ah", "ATh", "aot"):
                ab.free(r)
            af.free("qc")

            if stop_after in ("cast", "qkv", "chains", "attn"):
                for r in ("wo_skip",):
                    pass
                ab.free("xT")
                for rr in list(ab.live):
                    if rr not in ("xT",):
                        pass
                break
            wo = ab.alloc("wo", KC * D)
            self.load_reg(wo, _img_kpn(np.asarray(w["L0_ffn_Wo"][li], np.float32)))
            aoi = ab.alloc("aoi", 2 * 1024)
            aoT = ab.alloc("aoT", 1024)
            for ti in range(NT):
                sl = (ti % 2) * 1024
                P.dma("p", aoi.ap(sl, 1024), aos.rearrange("(t p) d -> p t d", p=128)[:, ti, :],
                      reads=[("aos", ti)], writes=[(rname(aoi), ti % 2)])
                P.dma("y", aoT.ap(0, None, dims=[[128, 8], [1, 128]]), aoi.ap(sl, 1024),
                      reads=[(rname(aoi), ti % 2)], writes=[rname(aoT)], transpose=True)
                pairs = []
                for n2 in range(2):
                    pairs = [(aoT.ap(k * 128, 128), wo.ap(k * D + n2 * 512, 512))
                             for k in range(KC)]
                    self.mm(self.ps[:, n2 * 512:(n2 + 1) * 512], [n2], pairs,
                            extra_reads=[rname(aoT), rname(wo)])
                self.ev("v", lambda E, ti=ti: E.tensor_tensor(
                    out=self.x[:, ti * D:(ti + 1) * D], in0=self.ps[:, 0:1024],
                    in1=self.x[:, ti * D:(ti + 1) * D], op=ALU.add),
                    reads=[("PS", 0), ("PS", 1), ("x", ti)], writes=[("x", ti)])
            ab.free("wo")
            ab.free("aoi")
            ab.free("aoT")

            if stop_after == "b2":
                ab.free("xT")
                break
            xb2 = ab.alloc("xb", NT * D)
            self.cast_x_bf16(xb2, eng="s")
            self.transpose_fm(xT, xb2)
            ab.free("xb")
            csq_scr = af.alloc("csq", 512)
            for ti in range(NT):
                xs = self.x[:, ti * D:(ti + 1) * D]
                self.ev("v", lambda E, xs=xs, ti=ti: E.tensor_reduce(
                    out=bass.AP(st2[:].tensor, 64 + ti, [[S2W, 128], [16, 2]]),
                    in_=xs.rearrange("p (h d) -> p h d", h=2), axis=AX.X, op=ALU.add),
                    reads=[("x", ti)], writes=[("stat2", ("s1", ti)), ("stat2", ("s2", ti))])
                for h in range(2):
                    self.ev("s", lambda E, xs=xs, ti=ti, h=h: E.activation(
                        out=csq_scr.ap(0, 512), in_=xs[:, h * 512:(h + 1) * 512],
                        func=AF.Square, accum_out=st2[:, 96 + h * 16 + ti:97 + h * 16 + ti]),
                        reads=[("x", ti)],
                        writes=[("stat2", ("ss" + str(h + 1), ti)), rname(csq_scr)])
            af.free("csq")
            for ti in range(NT):
                def sh_build(E, ti=ti):
                    rhs = bass.AP(st2[:].tensor, 80 + ti, [[S2W, 128], [32, 2]])
                    inst = E.matmul(self.ps[:, 7 * 512:7 * 512 + 2], smats[:], rhs,
                                    start=True, stop=(ti == 0))
                    if ti > 0:
                        rhp = bass.AP(st2[:].tensor, 80 + ti - 1, [[S2W, 128], [32, 2]])
                        inst = E.matmul(self.ps[:, 7 * 512:7 * 512 + 2], emats[:], rhp,
                                        start=False, stop=True)
                    return inst
                rd = [("stat2", ("s2", ti)), ("stat2", ("ss2", ti))]
                if ti > 0:
                    rd += [("stat2", ("s2", ti - 1)), ("stat2", ("ss2", ti - 1))]
                self.mm_raw(sh_build, banks=[7], extra_reads=rd + ["smat", "emat"])
                self.ev("v", lambda E, ti=ti: E.tensor_copy(
                    out=bass.AP(st2[:].tensor, 128 + ti, [[S2W, 128], [16, 2]]),
                    in_=self.ps[:, 7 * 512:7 * 512 + 2]),
                    reads=[("PS", 7)],
                    writes=[("stat2", ("s2s", ti)), ("stat2", ("ss2s", ti))])
            self.ev("v", lambda E: E.tensor_tensor(out=s2ap(160, 16), in0=s2ap(64, 16),
                    in1=s2ap(128, 16), op=ALU.add),
                    reads=[("stat2", ("s1", ti)) for ti in range(NT)]
                          + [("stat2", ("s2s", ti)) for ti in range(NT)],
                    writes=[("stat2", "cm0")])
            self.ev("v", lambda E: E.tensor_scalar(out=s2ap(160, 16), in0=s2ap(160, 16),
                    scalar1=1.0 / D, scalar2=None, op0=ALU.mult),
                    reads=[("stat2", "cm0")], writes=[("stat2", "cmean")])
            self.ev("v", lambda E: E.tensor_tensor(out=s2ap(176, 16), in0=s2ap(96, 16),
                    in1=s2ap(144, 16), op=ALU.add),
                    reads=[("stat2", ("ss1", ti)) for ti in range(NT)]
                          + [("stat2", ("ss2s", ti)) for ti in range(NT)],
                    writes=[("stat2", "cv0")])
            self.ev("v", lambda E: E.tensor_scalar(out=s2ap(176, 16), in0=s2ap(176, 16),
                    scalar1=1.0 / D, scalar2=None, op0=ALU.mult),
                    reads=[("stat2", "cv0")], writes=[("stat2", "cv1")])
            self.ev("v", lambda E: E.tensor_tensor(out=s2ap(208, 16), in0=s2ap(160, 16),
                    in1=s2ap(160, 16), op=ALU.mult),
                    reads=[("stat2", "cmean")], writes=[("stat2", "cmsq")])
            self.ev("v", lambda E: E.tensor_tensor(out=s2ap(176, 16), in0=s2ap(176, 16),
                    in1=s2ap(208, 16), op=ALU.subtract),
                    reads=[("stat2", "cv1"), ("stat2", "cmsq")], writes=[("stat2", "cvar")])
            self.ev("s", lambda E: E.activation(out=s2ap(208, 16), in_=s2ap(176, 16),
                    func=AF.Ln, bias=self.cst[:, 0:1]),
                    reads=[("stat2", "cvar"), ("cst", "eps")], writes=[("stat2", "clv")])
            self.ev("s", lambda E: E.activation(out=s2ap(192, 16), in_=s2ap(208, 16),
                    func=AF.Exp, scale=-0.5),
                    reads=[("stat2", "clv")], writes=[("stat2", "crstd")])
            self.ev("v", lambda E: E.tensor_scalar(out=s2ap(176, 16), in0=s2ap(176, 16),
                    scalar1=self.cst[:, 0:1], scalar2=None, op0=ALU.add),
                    reads=[("stat2", "cvar"), ("cst", "eps")], writes=[("stat2", "cvar")])
            self.newton_rsqrt(s2ap(192, 16), s2ap(176, 16), 16,
                              ("stat2", "crstd"), ("stat2", "cvar"),
                              s2ap(256, 16), ("stat2", "nsc"))
            mb = ab.alloc("mb", 2048, off=32896)
            rb = ab.alloc("rb", 2048, off=34944)
            stg = ab.alloc("stg", 128, off=36992)
            stgT = ab.alloc("stgT", 128, off=37120)
            selr = ab.alloc("selr", 2048, off=37248)
            if not hasattr(self, "_sel_img"):
                sel = np.zeros((16, 16 * 128), np.float32)
                for t_ in range(16):
                    sel[t_, t_ * 128:(t_ + 1) * 128] = 1.0
                self._sel_img = sel
            simg = self.inp(f"c_sel_{self.uid()}", self._sel_img)
            P.dma("p", ab.t[0:16, selr.off:selr.off + 2048], simg, writes=[rname(selr)])
            P.dma("p", ab.t[32:48, selr.off:selr.off + 2048], simg,
                  writes=[(rname(selr), "hi")])
            self.ev("v", lambda E: E.tensor_copy(
                out=ab.t[:, stg.off:stg.off + 16], in_=s2ap(160, 16)),
                reads=[("stat2", "cmean")], writes=[(rname(stg), "m")])
            self.ev("v", lambda E: E.tensor_copy(
                out=ab.t[:, stg.off + 32:stg.off + 48], in_=s2ap(192, 16)),
                reads=[("stat2", "crstd")], writes=[(rname(stg), "r")])
            P.dma("y", stgT.ap(0, None, dims=[[128, 1], [1, 128]]), stg.ap(0, 128),
                  reads=[(rname(stg), "m"), (rname(stg), "r")], writes=[rname(stgT)],
                  transpose=True)
            for ti in range(NT):
                def bc_build(E, ti=ti):
                    E.matmul(self.ps[:, 7 * 512:7 * 512 + 128],
                             ab.t[0:16, selr.off + ti * 128:selr.off + (ti + 1) * 128],
                             ab.t[0:16, stgT.off:stgT.off + 128], start=True, stop=True)
                    return E.matmul(self.ps[:, 6 * 512:6 * 512 + 128],
                                    ab.t[32:48, selr.off + ti * 128:selr.off + (ti + 1) * 128],
                                    ab.t[32:48, stgT.off:stgT.off + 128], start=True, stop=True)
                self.mm_raw(bc_build, banks=[6, 7],
                            extra_reads=[rname(selr), (rname(selr), "hi"), rname(stgT)])
                self.ev("v", lambda E, ti=ti: E.tensor_copy(
                    out=mb.ap(ti * 128, 128), in_=self.ps[:, 7 * 512:7 * 512 + 128]),
                    reads=[("PS", 7)], writes=[(rname(mb), ti)])
                self.ev("s", lambda E, ti=ti: E.copy(
                    out=rb.ap(ti * 128, 128), in_=self.ps[:, 6 * 512:6 * 512 + 128]),
                    reads=[("PS", 6)], writes=[(rname(rb), ti)])
            ab.free("stg")
            ab.free("stgT")
            ab.free("selr")
            xnT = ab.alloc("xnT", KC * 2048, off=0)
            c1 = af.alloc("c1", 1024)
            c2 = af.alloc("c2", 1024)
            mb_reads = [(rname(mb), ti) for ti in range(NT)]
            rb_reads = [(rname(rb), ti) for ti in range(NT)]
            for kc in range(KC):
                for hf in range(2):
                    self.ev("v", lambda E, kc=kc, hf=hf: E.tensor_tensor(
                        out=c1.ap(0, 1024), in0=xT.ap(shifted_col(kc, hf * 1024), 1024),
                        in1=mb.ap(hf * 1024, 1024), op=ALU.subtract),
                        reads=[(rname(xT), ("t", tt)) for tt in range(NT)] + mb_reads,
                        writes=[rname(c1)])
                    self.ev("v", lambda E, kc=kc, hf=hf: E.scalar_tensor_tensor(
                        out=c2.ap(0, 1024), in0=c1.ap(0, 1024), scalar=lngc[:, kc:kc + 1],
                        in1=rb.ap(hf * 1024, 1024), op0=ALU.mult, op1=ALU.mult),
                        reads=[rname(c1), "lngc"] + rb_reads, writes=[rname(c2)])
                    self.ev("v", lambda E, kc=kc, hf=hf: E.tensor_scalar(
                        out=xnT.ap(kc * 2048 + hf * 1024, 1024), in0=c2.ap(0, 1024),
                        scalar1=lnbc[:, kc:kc + 1], scalar2=None, op0=ALU.add),
                        reads=[rname(c2), "lnbc"], writes=[(rname(xnT), kc)])
            af.free("c2")
            if stop_after == "dumpxnT":
                dmp = P.dram_out("d_xnT", [128, KC * 2048], F32)
                dmpf = af.alloc("dmpf", 1024)
                for off in range(0, KC * 2048, 1024):
                    self.ev("s", lambda E, off=off: E.copy(out=dmpf.ap(0, 1024),
                            in_=xnT.ap(off, 1024)),
                            reads=[(rname(xnT), off // 2048)], writes=[rname(dmpf)])
                    dep = P.dma("y", dmp[:, off:off + 1024], dmpf.ap(0, 1024),
                                reads=[rname(dmpf)])
                    P.final_wait([dep])
                dmp2 = P.dram_out("d_mb", [128, 4096], F32)
                for j, reg in enumerate((mb, rb)):
                    for off in range(0, 2048, 1024):
                        self.ev("s", lambda E, off=off, reg=reg: E.copy(out=dmpf.ap(0, 1024),
                                in_=reg.ap(off, 1024)),
                                reads=[(rname(reg), (off // 128) + 0) if False else (rname(reg), None)],
                                writes=[rname(dmpf)])
                        dep = P.dma("y", dmp2[:, j * 2048 + off:j * 2048 + off + 1024],
                                    dmpf.ap(0, 1024), reads=[rname(dmpf)])
                        P.final_wait([dep])
                ab.free("mb")
                ab.free("rb")
                ab.free("xT")
                ab.free("xnT")
                af.free("c1")
                af.free("dmpf")
                break
            ab.free("mb")
            ab.free("rb")
            ab.free("xT")
            w1p = ab.alloc("w1p", 4 * 1024, off=16384)
            PT = ab.alloc("PT", NP * 512, off=20480)
            w2r = ab.alloc("w2r", NP * D, off=31744)
            self.load_reg(w2r, _img_kpn(W2pads[li], kc=NP))
            w1img = self.inp(f"c_w1p_{li}", np.ascontiguousarray(np.concatenate(
                [_img_kpn(W1pads[li][:, j * 128:(j + 1) * 128]) for j in range(2 * NP)], axis=1)))
            xnT_reads = [(rname(xnT), kc) for kc in range(KC)]
            for tb in range(4):
                for mp in range(NP):
                    slot = (mp % 2) * 2
                    P.dma("p", w1p.ap(slot * 1024, 1024), w1img[:, mp * 1024:(mp + 1) * 1024],
                          writes=[(rname(w1p), ("a", mp % 2))])
                    P.dma("p", w1p.ap(slot * 1024 + 1024, 1024),
                          w1img[:, (NP + mp) * 1024:(NP + mp + 1) * 1024],
                          writes=[(rname(w1p), ("g", mp % 2))])
                    bk = (mp % 2) * 2
                    pairs = [(w1p.ap(slot * 1024 + k * 128, 128),
                              xnT.ap(k * 2048 + tb * 512, 512)) for k in range(KC)]
                    self.mm(self.ps[:, bk * 512:(bk + 1) * 512], [bk], pairs,
                            extra_reads=[(rname(w1p), ("a", mp % 2))] + xnT_reads)
                    pairs = [(w1p.ap(slot * 1024 + 1024 + k * 128, 128),
                              xnT.ap(k * 2048 + tb * 512, 512)) for k in range(KC)]
                    self.mm(self.ps[:, (bk + 1) * 512:(bk + 2) * 512], [bk + 1], pairs,
                            extra_reads=[(rname(w1p), ("g", mp % 2))] + xnT_reads)
                    self.ev("s", lambda E, bk=bk, mp=mp: E.activation(
                        out=c1.ap(0, 512), in_=self.ps[:, (bk + 1) * 512:(bk + 2) * 512],
                        func=AF.Gelu, bias=b1c[:, NP + mp:NP + mp + 1]),
                        reads=[("PS", bk + 1), "b1c"], writes=[rname(c1)])
                    self.ev("v", lambda E, bk=bk, mp=mp: E.scalar_tensor_tensor(
                        out=PT.ap(mp * 512, 512), in0=self.ps[:, bk * 512:(bk + 1) * 512],
                        scalar=b1c[:, mp:mp + 1], in1=c1.ap(0, 512),
                        op0=ALU.add, op1=ALU.mult),
                        reads=[("PS", bk), rname(c1), "b1c"],
                        writes=[(rname(PT), mp)])
                if stop_after == "dumpPT" and tb == 3:
                    dmp = P.dram_out("d_PT", [128, NP * 512], F32)
                    dpf = af.alloc("dpf", 512)
                    for off in range(0, NP * 512, 512):
                        self.ev("s", lambda E, off=off: E.copy(out=dpf.ap(0, 512),
                                in_=PT.ap(off, 512)),
                                reads=[(rname(PT), off // 512)], writes=[rname(dpf)])
                        dep = P.dma("y", dmp[:, off:off + 512], dpf.ap(0, 512),
                                    reads=[rname(dpf)])
                        P.final_wait([dep])
                    af.free("dpf")
                for mt in range(4):
                    ti = tb * 4 + mt

                    def w2_build(E, mt=mt):
                        inst = None
                        for n2 in range(2):
                            for k in range(NP):
                                inst = E.matmul(
                                    self.ps[:, (mt * 2 + n2) * 512:(mt * 2 + n2 + 1) * 512],
                                    ab.t[:, PT.off + k * 512 + mt * 128:PT.off + k * 512 + (mt + 1) * 128],
                                    ab.t[:, w2r.off + k * D + n2 * 512:w2r.off + k * D + n2 * 512 + 512],
                                    start=(k == 0), stop=False)
                            inst = E.matmul(
                                self.ps[:, (mt * 2 + n2) * 512:(mt * 2 + n2 + 1) * 512],
                                self.ones_row[0:1, :], b2row[0:1, n2 * 512:(n2 + 1) * 512],
                                start=False, stop=True)
                        return inst
                    self.mm_raw(w2_build, banks=[mt * 2, mt * 2 + 1],
                                extra_reads=[(rname(PT), k) for k in range(NP)]
                                            + [rname(w2r), "ones_row", "b2row"])
                    self.ev("v", lambda E, ti=ti, mt=mt: E.tensor_tensor(
                        out=self.x[:, ti * D:(ti + 1) * D],
                        in0=self.ps[:, mt * 1024:(mt + 1) * 1024],
                        in1=self.x[:, ti * D:(ti + 1) * D], op=ALU.add),
                        reads=[("PS", mt * 2), ("PS", mt * 2 + 1), ("x", ti)],
                        writes=[("x", ti)])
            ab.free("w1p")
            ab.free("PT")
            ab.free("w2r")
            ab.free("xnT")
            af.free("c1")
        if not skip_final_ln:
            self.ln_x_inplace(add_hbm=xin_ap)


MK.l0_ffn = L0FFNMixin.l0_ffn


class L1Mixin:
    def ln_affine_to_fm(self, dst_fm, gb, bb, htile, cw=2048, pad=0, eps=1e-5):
        P, af, st = self.P, self.af, self.stat
        SM, SS, MSQ, LV, RSTD = 0, 32, 64, 80, 96
        lsq = af.alloc("lnsq2", D)
        tsc = af.alloc("lnt2", D)
        for ti in range(NT):
            xs = self.x[:, ti * D:(ti + 1) * D]
            self.ev("s", lambda E, xs=xs, ti=ti: E.activation(
                out=lsq.ap(0, D), in_=xs, func=AF.Square,
                accum_out=st[:, SS + ti:SS + ti + 1]),
                reads=[("x", ti)], writes=[("stat", ("ss", ti)), rname(lsq)])
            self.ev("v", lambda E, xs=xs, ti=ti: E.tensor_reduce(
                out=st[:, SM + ti:SM + ti + 1], in_=xs, axis=AX.X, op=ALU.add),
                reads=[("x", ti)], writes=[("stat", ("sm", ti))])
        rs = [("stat", ("sm", ti)) for ti in range(NT)]
        rss = [("stat", ("ss", ti)) for ti in range(NT)]
        n = NT
        self.ev("v", lambda E: E.tensor_scalar(out=st[:, SM:SM + n], in0=st[:, SM:SM + n],
                scalar1=1.0 / D, scalar2=None, op0=ALU.mult), reads=rs, writes=rs)
        self.ev("v", lambda E: E.tensor_tensor(out=st[:, MSQ:MSQ + n], in0=st[:, SM:SM + n],
                in1=st[:, SM:SM + n], op=ALU.mult), reads=rs, writes=[("stat", "msq")])
        self.ev("v", lambda E: E.tensor_scalar(out=st[:, SS:SS + n], in0=st[:, SS:SS + n],
                scalar1=1.0 / D, scalar2=None, op0=ALU.mult), reads=rss, writes=rss)
        self.ev("v", lambda E: E.tensor_tensor(out=st[:, SS:SS + n], in0=st[:, SS:SS + n],
                in1=st[:, MSQ:MSQ + n], op=ALU.subtract),
                reads=[("stat", "msq")] + rss, writes=rss)
        self.ev("s", lambda E: E.activation(out=st[:, LV:LV + n], in_=st[:, SS:SS + n],
                func=AF.Ln, bias=self.cst[:, 0:1]), reads=rss + [("cst", "eps")],
                writes=[("stat", "lv")])
        self.ev("s", lambda E: E.activation(out=st[:, RSTD:RSTD + n], in_=st[:, LV:LV + n],
                func=AF.Exp, scale=-0.5), reads=[("stat", "lv")],
                writes=[("stat", ("rstd", ti)) for ti in range(NT)])
        for ti in range(NT):
            xs = self.x[:, ti * D:(ti + 1) * D]
            self.ev("v", lambda E, xs=xs, ti=ti: E.tensor_scalar(
                out=tsc.ap(0, D), in0=xs, scalar1=st[:, SM + ti:SM + ti + 1],
                scalar2=st[:, RSTD + ti:RSTD + ti + 1], op0=ALU.subtract, op1=ALU.mult),
                reads=[("x", ti), ("stat", ("sm", ti)), ("stat", ("rstd", ti))],
                writes=[rname(tsc)])
            self.ev("p", lambda E: E.tensor_tensor(out=tsc.ap(0, D), in0=tsc.ap(0, D),
                    in1=gb.ap(0, D), op=ALU.mult),
                    reads=[rname(tsc), rname(gb)], writes=[rname(tsc)])
            self.ev("v", lambda E, ti=ti: E.tensor_tensor(out=htile.ap(0, D),
                    in0=tsc.ap(0, D), in1=bb.ap(0, D), op=ALU.add),
                    reads=[rname(tsc), rname(bb)], writes=[rname(htile)])
            self.P.dma("y", dst_fm.ap(pad + ti * 128, None, dims=[[cw, KC], [1, 128]]),
                       htile.ap(0, D), reads=[rname(htile)],
                       writes=[(rname(dst_fm), ("t", ti))], transpose=True)
        af.free("lnsq2")
        af.free("lnt2")

    def bcast_row(self, dst, rowreg, n=D):
        ab = self.ab
        for nb in range((n + 511) // 512):
            w = min(512, n - nb * 512)
            self.mm(self.ps[:, 7 * 512:7 * 512 + w], [7],
                    [(self.ones_row[0:1, :],
                      ab.t[0:1, rowreg.off + nb * 512:rowreg.off + nb * 512 + w])],
                    extra_reads=["ones_row", rname(rowreg)])
            self.ev("v", lambda E, nb=nb, w=w: E.tensor_copy(
                out=dst.ap(nb * 512, w), in_=self.ps[:, 7 * 512:7 * 512 + w]),
                reads=[("PS", 7)], writes=[(rname(dst), nb)])

    def l1_attn(self, w, xin_ap):
        P, ab, af = self.P, self.ab, self.af
        P.barrier_name("stat")
        P.barrier_name("stat2")
        st2 = self.stat2 = self.P.sb.get("stat2") or P.sbuf("stat2", [128, 288], F32)
        S2W = 288

        def s2ap(col, n=1, bc=None):
            if bc is None:
                return st2[:, col:col + n]
            return bass.AP(st2[:].tensor, col, [[S2W, 128], [1, n], [0, bc]])
        H, DK, DV = 16, 32, 64
        def row_in(name, arr):
            rg = ab.alloc(name, 1024)
            P.dma("p", ab.t[0:1, rg.off:rg.off + arr.shape[0]],
                  self.inp("c_" + name + f"_{self.uid()}",
                  np.asarray(arr, np.float32)[None, :]), writes=[rname(rg)])
            return rg
        r_lnw = row_in("r_lnw", w["L1_attn_ln_w"])
        r_lnb = row_in("r_lnb", w["L1_attn_ln_b"])
        r_rmsw = row_in("r_rmsw", np.tile(np.asarray(w["L1_attn_rms_w"], np.float32), 16))
        onesf = P.sb.get("onesf") or P.sbuf("onesf", [128, 128], F32)
        self.ev("v", lambda E: E.memset(onesf[:], 1.0), writes=["onesf"])
        self.ev("v", lambda E: E.memset(self.cst[:, 2:3], 1.0), writes=[("cst", "one")])
        tri = np.tril(np.ones((128, 128), np.float32)).T
        trit = self.load_sbuf("tri", np.ascontiguousarray(tri))
        gkb2c = P.sbuf("gkb2c", [128, 4], F32)
        P.dma("p", gkb2c[:], self.inp("c_gkb2", np.ascontiguousarray(
            np.asarray(w["L1_attn_gk_b2"], np.float32).reshape(4, 128).T)), writes=["gkb2c"])
        b1c2 = P.sbuf("b1c2", [128, 16], F32)
        P.dma("p", b1c2[:], self.inp("c_l1b1", np.ascontiguousarray(
            np.asarray(w["L1_attn_b1"], np.float32).reshape(16, 128).T)), writes=["b1c2"])
        t_b2 = P.sb.get("b2row") or P.sbuf("b2row", [1, D], BF16)
        P.dma("p", t_b2[:], self.inp(f"c_r_l1b2_{self.uid()}",
              np.asarray(w["L1_attn_b2"], np.float32)[None, :]), writes=["b2row"])
        gkw1r = ab.alloc("gkw1", 128)
        P.dma("p", ab.t[:, gkw1r.off:gkw1r.off + 128],
              self.inp("c_gkw1", _img_kpn(np.asarray(w["L1_attn_gk_w1"], np.float32))),
              writes=[rname(gkw1r)])
        gkw2r = ab.alloc("gkw2", 512)
        P.sbuf("t1b", [16, 128], BF16)
        P.dma("p", ab.t[0:16, gkw2r.off:gkw2r.off + 512],
              self.inp("c_gkw2", np.asarray(w["L1_attn_gk_w2"], np.float32)),
              writes=[rname(gkw2r)])
        aos = self.dram_scratch_bf16()
        hT = ab.alloc("hT", KC * 2048)
        gb = ab.alloc("gb", D)
        bb = ab.alloc("bb", D)
        self.bcast_row(gb, r_lnw)
        self.bcast_row(bb, r_lnb)
        rmsb = ab.alloc("rmsb", D)
        self.bcast_row(rmsb, r_rmsw)
        ab.free("r_lnw")
        ab.free("r_lnb")
        ab.free("r_rmsw")
        htile = ab.alloc("htile", 1024)
        self.ln_affine_to_fm(hT, gb, bb, htile)
        ab.free("htile")
        wq = ab.alloc("wq", KC * 512)
        self.load_reg(wq, _img_kpn(np.asarray(w["L1_attn_Wq"], np.float32)))
        wk = ab.alloc("wk", KC * 512)
        self.load_reg(wk, _img_kpn(np.asarray(w["L1_attn_Wk"], np.float32)))
        wv = ab.alloc("wv", KC * D)
        self.load_reg(wv, _img_kpn(np.asarray(w["L1_attn_Wv"], np.float32)))
        wg = ab.alloc("wg", KC * D)
        self.load_reg(wg, _img_kpn(np.asarray(w["L1_attn_Wg"], np.float32)))
        qtb = ab.alloc("qtb", 512)
        ktb = ab.alloc("ktb", 512)
        kttok = ab.alloc("kttok", 512)
        ptb = ab.alloc("ptb", 2048)
        vcb = ab.alloc("vcb", 1024)
        gsb = ab.alloc("gsb", 1024)
        ochunk = ab.alloc("ochunk", 1024)
        Sb = ab.alloc("Sb", 1024)
        Sst = af.alloc("Sst", 1024)
        zch = af.alloc("zch", 1024)
        Ach = af.alloc("Ach", 512)
        ech = af.alloc("ech", 512)
        self.ev("v", lambda E: E.memset(Sb.ap(0, 1024), 0.0), writes=[rname(Sb)])
        self.ev("v", lambda E: E.memset(af.t[:, Sst.off:Sst.off + 1024], 0.0),
                writes=[rname(Sst)])
        hT_reads = [(rname(hT), ("t", tt)) for tt in range(NT)]
        for c in range(NT):
            hcol = c * 128
            for fm in range(4):
                pairs = [(wq.ap(k * 512 + fm * 128, 128), hT.ap(k * 2048 + hcol, 128))
                         for k in range(KC)]
                self.mm(self.ps[:, 0 * 512 + fm * 128:0 * 512 + (fm + 1) * 128], [0], pairs,
                        extra_reads=[rname(wq)] + hT_reads)
                pairs = [(wk.ap(k * 512 + fm * 128, 128), hT.ap(k * 2048 + hcol, 128))
                         for k in range(KC)]
                self.mm(self.ps[:, 1 * 512 + fm * 128:1 * 512 + (fm + 1) * 128], [1], pairs,
                        extra_reads=[rname(wk)] + hT_reads)
            pairs = [(ab.t[:, gkw1r.off + k * 16:gkw1r.off + (k + 1) * 16],
                      hT.ap(k * 2048 + hcol, 128)) for k in range(KC)]
            self.mm(self.ps[0:16, 2 * 512:2 * 512 + 128], [2], pairs,
                    extra_reads=[rname(gkw1r)] + hT_reads)
            self.ev("v", lambda E: E.tensor_copy(out=P.sb["t1b"][0:16, :],
                    in_=self.ps[0:16, 2 * 512:2 * 512 + 128]),
                    reads=[("PS", 2)], writes=["t1b"])
            for fm in range(4):
                self.mm(self.ps[:, 3 * 512 + fm * 128:3 * 512 + (fm + 1) * 128], [3],
                        [(ab.t[0:16, gkw2r.off + fm * 128:gkw2r.off + (fm + 1) * 128],
                          P.sb["t1b"][0:16, :])],
                        extra_reads=[rname(gkw2r), "t1b"])
            self.ev("v", lambda E: E.tensor_tensor(
                out=zch.ap(0, None, dims=[[128, 4], [1, 128]]),
                in0=bass.AP(self.ps[:].tensor, 3 * 512, [[4096, 128], [128, 4], [1, 128]]),
                in1=bass.AP(gkb2c[:].tensor, 0, [[4, 128], [1, 4], [0, 128]]),
                op=ALU.add),
                reads=[("PS", 3), "gkb2c"], writes=[rname(zch)])
            self.ev("s", lambda E: E.activation(out=ech.ap(0, 512), in_=zch.ap(0, 512),
                    func=AF.Abs), reads=[rname(zch)], writes=[rname(ech)])
            self.ev("s", lambda E: E.activation(out=ech.ap(0, 512), in_=ech.ap(0, 512),
                    func=AF.Exp, scale=-1.0), reads=[rname(ech)], writes=[rname(ech)])
            self.ev("s", lambda E: E.activation(out=ech.ap(0, 512), in_=ech.ap(0, 512),
                    func=AF.Ln, bias=self.cst[:, 2:3]),
                    reads=[rname(ech), ("cst", "one")], writes=[rname(ech)])
            self.ev("v", lambda E: E.tensor_scalar(out=zch.ap(0, 512), in0=zch.ap(0, 512),
                    scalar1=0.0, scalar2=None, op0=ALU.min),
                    reads=[rname(zch)], writes=[rname(zch)])
            self.ev("v", lambda E: E.tensor_tensor(out=zch.ap(0, 512), in0=zch.ap(0, 512),
                    in1=ech.ap(0, 512), op=ALU.subtract),
                    reads=[rname(zch), rname(ech)], writes=[rname(zch)])
            for fm in range(4):
                self.ev("v", lambda E, fm=fm: E.tensor_tensor_scan(
                    out=Ach.ap(fm * 128, 128), data0=zch.ap(fm * 128, 128),
                    data1=onesf[:, 0:128], initial=0.0, op0=ALU.add, op1=ALU.mult),
                    reads=[rname(zch), "onesf"], writes=[rname(Ach)])
            self.ev("s", lambda E: E.activation(
                out=s2ap(276, 4), in_=bass.AP(af.t[:].tensor, Ach.off + 127, [[af.words, 128], [128, 4]]),
                func=AF.Exp, scale=1.0 / 16), reads=[rname(Ach)], writes=[("stat2", "eac")])
            self.ev("s", lambda E: E.activation(out=ech.ap(0, 512), in_=Ach.ap(0, 512),
                    func=AF.Exp, scale=1.0 / 16), reads=[rname(Ach)], writes=[rname(ech)])
            self.ev("v", lambda E: E.scalar_tensor_tensor(out=qtb.ap(0, 512),
                    in0=self.ps[:, 0:512], scalar=float(DK) ** -0.5, in1=ech.ap(0, 512),
                    op0=ALU.mult, op1=ALU.mult),
                    reads=[("PS", 0), rname(ech)], writes=[rname(qtb)])
            self.ev("s", lambda E: E.activation(out=ech.ap(0, 512), in_=Ach.ap(0, 512),
                    func=AF.Exp, scale=-1.0 / 16), reads=[rname(Ach)], writes=[rname(ech)])
            self.ev("v", lambda E: E.tensor_tensor(out=ktb.ap(0, 512),
                    in0=self.ps[:, 512:1024], in1=ech.ap(0, 512), op=ALU.mult),
                    reads=[("PS", 1), rname(ech)], writes=[rname(ktb)])
            P.dma("y", kttok.ap(0, None, dims=[[128, 4], [1, 128]]), ktb.ap(0, 512),
                  reads=[rname(ktb)], writes=[rname(kttok)], transpose=True)
            for nb in range(2):
                pairs = [(hT.ap(k * 2048 + hcol, 128), wv.ap(k * D + nb * 512, 512))
                         for k in range(KC)]
                self.mm(self.ps[:, (4 + nb) * 512:(5 + nb) * 512], [4 + nb], pairs,
                        extra_reads=[rname(wv)] + hT_reads)
            self.ev("v", lambda E: E.tensor_copy(out=vcb.ap(0, 1024),
                    in_=self.ps[:, 4 * 512:6 * 512]),
                    reads=[("PS", 4), ("PS", 5)], writes=[rname(vcb)])
            for nb in range(2):
                pairs = [(hT.ap(k * 2048 + hcol, 128), wg.ap(k * D + nb * 512, 512))
                         for k in range(KC)]
                self.mm(self.ps[:, (4 + nb) * 512:(5 + nb) * 512], [4 + nb], pairs,
                        extra_reads=[rname(wg)] + hT_reads)
            self.ev("s", lambda E: E.activation(out=gsb.ap(0, 1024),
                    in_=self.ps[:, 4 * 512:6 * 512], func=AF.Silu),
                    reads=[("PS", 4), ("PS", 5)], writes=[rname(gsb)])
            def pt_build(E):
                inst = None
                for h in range(16):
                    m = h % 4
                    fm = h // 4
                    p0 = m * 32
                    lhsT = ab.t[p0:p0 + 32, ktb.off + fm * 128:ktb.off + (fm + 1) * 128]
                    rhs = ab.t[p0:p0 + 32, qtb.off + fm * 128:qtb.off + (fm + 1) * 128]
                    inst = E.matmul(self.ps[:, (2 + m) * 512 + fm * 128:(2 + m) * 512 + (fm + 1) * 128],
                                    lhsT, rhs, start=True, stop=True,
                                    tile_position=(p0, 0))
                return inst
            self.mm_raw(pt_build, banks=[2, 3, 4, 5], extra_reads=[rname(ktb), rname(qtb)])
            self.ev("v", lambda E: E.tensor_tensor(
                out=ptb.ap(0, None, dims=[[128, 16], [1, 128]]),
                in0=bass.AP(self.ps[:].tensor, 2 * 512, [[4096, 128], [128, 16], [1, 128]]),
                in1=bass.AP(trit[:].tensor, 0, [[128, 128], [0, 16], [1, 128]]),
                op=ALU.mult),
                reads=[("PS", 2), ("PS", 3), ("PS", 4), ("PS", 5), "tri"],
                writes=[rname(ptb)])
            def o_build(E):
                inst = None
                for h in range(16):
                    m, fm = h % 4, h // 4
                    pcol = (2 + m) * 512 + fm * 128
                    lhsT = ab.t[:, ptb.off + (m * 4 + fm) * 128:ptb.off + (m * 4 + fm) * 128 + 128]
                    rhs = ab.t[:, vcb.off + h * 64:vcb.off + (h + 1) * 64]
                    inst = E.matmul(self.ps[:, 3072 + h * 64:3072 + (h + 1) * 64],
                                    lhsT, rhs, start=(h == 0 or h == 8), stop=False)
                for fm in range(4):
                    lhsT = ab.t[:, qtb.off + fm * 128:qtb.off + (fm + 1) * 128]
                    rhs = ab.t[:, Sb.off + fm * 256:Sb.off + (fm + 1) * 256]
                    inst = E.matmul(self.ps[:, 3072 + fm * 256:3072 + (fm + 1) * 256],
                                    lhsT, rhs, start=False, stop=True)
                return inst
            self.mm_raw(o_build, banks=[6, 7],
                        extra_reads=[rname(ptb), rname(vcb), rname(qtb), rname(Sb)])
            def u_build(E):
                inst = None
                for h in range(16):
                    fm = h // 4
                    lhsT = ab.t[:, kttok.off + fm * 128:kttok.off + (fm + 1) * 128]
                    rhs = ab.t[:, vcb.off + h * 64:vcb.off + (h + 1) * 64]
                    inst = E.matmul(self.ps[:, 2 * 512 + h * 64:2 * 512 + (h + 1) * 64],
                                    lhsT, rhs, start=True, stop=True)
                return inst
            self.mm_raw(u_build, banks=[2, 3], extra_reads=[rname(kttok), rname(vcb)])
            self.ev("v", lambda E: E.tensor_tensor(out=af.t[:, Sst.off:Sst.off + 1024],
                    in0=self.ps[:, 2 * 512:4 * 512], in1=af.t[:, Sst.off:Sst.off + 1024],
                    op=ALU.add), reads=[("PS", 2), ("PS", 3), rname(Sst)], writes=[rname(Sst)])
            self.ev("v", lambda E: E.tensor_tensor(
                out=bass.AP(af.t[:].tensor, Sst.off, [[af.words, 128], [256, 4], [1, 256]]),
                in0=bass.AP(af.t[:].tensor, Sst.off, [[af.words, 128], [256, 4], [1, 256]]),
                in1=s2ap(276, 4, bc=256), op=ALU.mult),
                reads=[rname(Sst), ("stat2", "eac")], writes=[rname(Sst)])
            for fm in range(4):
                for m in range(4):
                    self.ev("v", lambda E, fm=fm, m=m: E.tensor_copy(
                        out=ab.t[m * 32:(m + 1) * 32,
                                 Sb.off + fm * 256 + m * 64:Sb.off + fm * 256 + (m + 1) * 64],
                        in_=af.t[m * 32:(m + 1) * 32,
                                 Sst.off + (4 * fm + m) * 64:Sst.off + (4 * fm + m + 1) * 64]),
                        reads=[rname(Sst)], writes=[rname(Sb)])
            self.ev("s", lambda E: E.activation(out=zch.ap(0, 1024),
                    in_=self.ps[:, 3072:4096], func=AF.Square),
                    reads=[("PS", 6), ("PS", 7)], writes=[rname(zch)])
            self.ev("v", lambda E: E.tensor_reduce(out=st2[:, 244:260],
                    in_=zch.ap(0, None, dims=[[64, 16], [1, 64]]), axis=AX.X, op=ALU.add),
                    reads=[rname(zch)], writes=[("stat2", "oss")])
            self.ev("v", lambda E: E.tensor_scalar(out=st2[:, 244:260], in0=st2[:, 244:260],
                    scalar1=1.0 / 64, scalar2=self.cst[:, 0:1], op0=ALU.mult, op1=ALU.add),
                    reads=[("stat2", "oss"), ("cst", "eps")], writes=[("stat2", "oss")])
            self.ev("s", lambda E: E.activation(out=st2[:, 260:276], in_=st2[:, 244:260],
                    func=AF.Ln), reads=[("stat2", "oss")], writes=[("stat2", "olv")])
            self.ev("s", lambda E: E.activation(out=st2[:, 260:276], in_=st2[:, 260:276],
                    func=AF.Exp, scale=-0.5), reads=[("stat2", "olv")], writes=[("stat2", "orstd")])
            self.newton_rsqrt(st2[:, 260:276], st2[:, 244:260], 16,
                              ("stat2", "orstd"), ("stat2", "oss"),
                              st2[:, 224:240], ("stat2", "nsc2"))
            self.ev("v", lambda E: E.tensor_tensor(
                out=zch.ap(0, None, dims=[[64, 16], [1, 64]]),
                in0=bass.AP(self.ps[:].tensor, 3072, [[4096, 128], [64, 16], [1, 64]]),
                in1=bass.AP(st2[:].tensor, 260, [[S2W, 128], [1, 16], [0, 64]]),
                op=ALU.mult),
                reads=[("PS", 6), ("PS", 7), ("stat2", "orstd")], writes=[rname(zch)])
            self.ev("p", lambda E: E.tensor_tensor(out=zch.ap(0, 1024), in0=zch.ap(0, 1024),
                    in1=rmsb.ap(0, 1024), op=ALU.mult),
                    reads=[rname(zch), (rname(rmsb), None)], writes=[rname(zch)])
            self.ev("v", lambda E: E.tensor_tensor(out=ochunk.ap(0, 1024), in0=zch.ap(0, 1024),
                    in1=gsb.ap(0, 1024), op=ALU.mult),
                    reads=[rname(zch), rname(gsb)], writes=[rname(ochunk)])
            P.dma("y", aos.rearrange("(t p) d -> p t d", p=128)[:, c, :], ochunk.ap(0, 1024),
                  reads=[rname(ochunk)], writes=[("aos", c)])
        for r in ("wq", "wk", "wv", "wg", "qtb", "ktb", "kttok", "ptb", "vcb", "gsb",
                  "ochunk", "Sb", "hT", "gkw1", "gkw2"):
            ab.free(r)
        for r in ("Sst", "zch", "Ach", "ech"):
            af.free(r)
        wo = ab.alloc("wo", KC * D)
        self.load_reg(wo, _img_kpn(np.asarray(w["L1_attn_Wo"], np.float32)))
        aoi = ab.alloc("aoi", 2 * 1024)
        aoT = ab.alloc("aoT", 1024)
        for ti in range(NT):
            sl = (ti % 2) * 1024
            P.dma("p", aoi.ap(sl, 1024), aos.rearrange("(t p) d -> p t d", p=128)[:, ti, :],
                  reads=[("aos", ti)], writes=[(rname(aoi), ti % 2)])
            P.dma("y", aoT.ap(0, None, dims=[[128, 8], [1, 128]]), aoi.ap(sl, 1024),
                  reads=[(rname(aoi), ti % 2)], writes=[rname(aoT)], transpose=True)
            for n2 in range(2):
                pairs = [(aoT.ap(k * 128, 128), wo.ap(k * D + n2 * 512, 512))
                         for k in range(KC)]
                self.mm(self.ps[:, n2 * 512:(n2 + 1) * 512], [n2], pairs,
                        extra_reads=[rname(aoT), rname(wo)])
            self.ev("v", lambda E, ti=ti: E.tensor_tensor(
                out=self.x[:, ti * D:(ti + 1) * D], in0=self.ps[:, 0:1024],
                in1=self.x[:, ti * D:(ti + 1) * D], op=ALU.add),
                reads=[("PS", 0), ("PS", 1), ("x", ti)], writes=[("x", ti)])
        ab.free("wo")
        ab.free("aoi")
        ab.free("aoT")
        h2T = ab.alloc("h2T", KC * 2048, off=0)
        htile = ab.alloc("htile", 1024, off=16384)
        self.ln_affine_to_fm(h2T, gb, bb, htile)
        ab.free("htile")
        ab.free("gb")
        ab.free("bb")
        ab.free("rmsb")
        w1r = ab.alloc("w1r", KC * 2048, off=16384)
        self.load_reg(w1r, _img_kpn(np.asarray(w["L1_attn_W1"], np.float32)))
        w2r = ab.alloc("w2r", 16 * D, off=32768)
        self.load_reg(w2r, _img_kpn(np.asarray(w["L1_attn_W2"], np.float32), kc=16))
        HTb = ab.alloc("HTb", 16 * 512, off=49152)
        h2T_reads = [(rname(h2T), ("t", tt)) for tt in range(NT)]
        for tb in range(4):
            for hm in range(16):
                pairs = [(w1r.ap(k * 2048 + hm * 128, 128), h2T.ap(k * 2048 + tb * 512, 512))
                         for k in range(KC)]
                bk = hm % 4
                self.mm(self.ps[:, bk * 512:(bk + 1) * 512], [bk], pairs,
                        extra_reads=[rname(w1r)] + h2T_reads)
                self.ev("s", lambda E, bk=bk, hm=hm: E.activation(
                    out=HTb.ap(hm * 512, 512), in_=self.ps[:, bk * 512:(bk + 1) * 512],
                    func=AF.Gelu, bias=b1c2[:, hm:hm + 1]),
                    reads=[("PS", bk), "b1c2"], writes=[(rname(HTb), hm)])
            for mt in range(4):
                ti = tb * 4 + mt

                def w2_build(E, mt=mt):
                    inst = None
                    for n2 in range(2):
                        for k in range(16):
                            inst = E.matmul(
                                self.ps[:, (4 + mt % 2 * 2 + n2) * 512:(4 + mt % 2 * 2 + n2 + 1) * 512],
                                ab.t[:, HTb.off + k * 512 + mt * 128:HTb.off + k * 512 + (mt + 1) * 128],
                                ab.t[:, w2r.off + k * D + n2 * 512:w2r.off + k * D + n2 * 512 + 512],
                                start=(k == 0), stop=False)
                        inst = E.matmul(
                            self.ps[:, (4 + mt % 2 * 2 + n2) * 512:(4 + mt % 2 * 2 + n2 + 1) * 512],
                            self.ones_row[0:1, :], P.sb["b2row"][0:1, n2 * 512:(n2 + 1) * 512],
                            start=False, stop=True)
                    return inst
                bks = [4 + mt % 2 * 2, 5 + mt % 2 * 2]
                self.mm_raw(w2_build, banks=bks,
                            extra_reads=[(rname(HTb), k) for k in range(16)]
                                        + [rname(w2r), "ones_row", "b2row"])
                self.ev("v", lambda E, ti=ti, bks=bks: E.tensor_tensor(
                    out=self.x[:, ti * D:(ti + 1) * D],
                    in0=self.ps[:, bks[0] * 512:(bks[1] + 1) * 512],
                    in1=self.x[:, ti * D:(ti + 1) * D], op=ALU.add),
                    reads=[("PS", bks[0]), ("PS", bks[1]), ("x", ti)],
                    writes=[("x", ti)])
        ab.free("w1r")
        ab.free("w2r")
        ab.free("HTb")
        ab.free("h2T")
        self.ln_x_inplace(add_hbm=xin_ap)

    def dram_scratch_bf16(self):
        if not hasattr(self, "_aos"):
            if "aos" in self.P.dram:
                self._aos = self.P.dram["aos"]
            else:
                self._aos = self.P.nc.dram_tensor("aos", [T, D], BF16).ap()
                self.P.dram["aos"] = self._aos
        return self._aos


MK.ln_affine_to_fm = L1Mixin.ln_affine_to_fm
MK.bcast_row = L1Mixin.bcast_row
MK.l1_attn = L1Mixin.l1_attn
MK.dram_scratch_bf16 = L1Mixin.dram_scratch_bf16


class MoEMixin:
    def l1_ffn(self, w):
        P, ab, af = self.P, self.ab, self.af
        P.barrier_name("stat")
        P.barrier_name("stat2")
        st2 = self.stat2 = self.P.sb.get("stat2") or P.sbuf("stat2", [128, 288], F32)
        E_, F_ = 8, 1408
        NPF = 11
        gwf = P.sbuf("gatewf", [128, KC * 8], F32)
        P.dma("p", gwf[:], self.inp("c_gatew", _img_kpn(np.asarray(w["L1_ffn_gate_w"], np.float32))),
              writes=["gatewf"])
        gw = P.sbuf("gatew", [128, KC * 8], BF16)
        gwl = P.sbuf("gatewl", [128, KC * 8], BF16)
        self.ev("v", lambda E: E.tensor_copy(out=gw[:], in_=gwf[:]),
                reads=["gatewf"], writes=["gatew"])
        self.ev("v", lambda E: E.tensor_tensor(out=gwl[:], in0=gwf[:], in1=gw[:],
                op=ALU.subtract), reads=["gatewf", "gatew"], writes=["gatewl"])
        xbm = ab.alloc("xbm", NT * D)
        self.cast_x_bf16(xbm, eng="s")
        xmT = ab.alloc("xmT", KC * 2048)
        self.transpose_fm(xmT, xbm, cw=2048, pad=0)
        for i in range(4):
            self.ev("v", lambda E, i=i: E.tensor_tensor(
                out=xbm.ap(i * 4 * D, 4 * D), in0=self.x[:, i * 4 * D:(i + 1) * 4 * D],
                in1=xbm.ap(i * 4 * D, 4 * D), op=ALU.subtract),
                reads=[("x", tt) for tt in range(i * 4, i * 4 + 4)] + [(rname(xbm), ("cast", i))],
                writes=[(rname(xbm), ("cast", i))])
        xloT = ab.alloc("xloT", KC * 2048)
        self.transpose_fm(xloT, xbm, cw=2048, pad=0)
        ab.free("xbm")
        xmT_reads = [(rname(xmT), ("t", tt)) for tt in range(NT)]
        xloT_reads = [(rname(xloT), ("t", tt)) for tt in range(NT)]
        wn = af.alloc("wn", NT * 8)
        lg = af.alloc("lg", 16)
        for ti in range(NT):
            pairs = [(xmT.ap(k * 2048 + ti * 128, 128), gw[:, k * 8:(k + 1) * 8])
                     for k in range(KC)]
            pairs += [(xmT.ap(k * 2048 + ti * 128, 128), gwl[:, k * 8:(k + 1) * 8])
                      for k in range(KC)]
            pairs += [(xloT.ap(k * 2048 + ti * 128, 128), gw[:, k * 8:(k + 1) * 8])
                      for k in range(KC)]
            self.mm(self.ps[:, 7 * 512:7 * 512 + 8], [7], pairs,
                    extra_reads=["gatew", "gatewl"] + xmT_reads + xloT_reads)
            self.ev("v", lambda E: E.tensor_copy(out=lg.ap(0, 8), in_=self.ps[:, 7 * 512:7 * 512 + 8]),
                    reads=[("PS", 7)], writes=[rname(lg)])
            self.ev("v", lambda E: E.reduce_max(out=st2[:, 280:281], in_=lg.ap(0, 8),
                    axis=AX.X, negate=True), reads=[rname(lg)], writes=[("stat2", "rm1")])
            self.ev("s", lambda E: E.activation(out=lg.ap(8, 8), in_=lg.ap(0, 8),
                    func=AF.Exp, bias=st2[:, 280:281]),
                    reads=[rname(lg), ("stat2", "rm1")], writes=[rname(lg)])
            self.ev("v", lambda E: E.tensor_scalar(out=st2[:, 281:282], in0=st2[:, 280:281],
                    scalar1=-1.0, scalar2=None, op0=ALU.mult),
                    reads=[("stat2", "rm1")], writes=[("stat2", "rm1p")])
            self.ev("v", lambda E: E.tensor_scalar(out=lg.ap(0, 8), in0=lg.ap(0, 8),
                    scalar1=st2[:, 281:282], scalar2=None, op0=ALU.is_lt),
                    reads=[rname(lg), ("stat2", "rm1p")], writes=[(rname(lg), "m1mask")])
            self.ev("v", lambda E: E.tensor_tensor(out=lg.ap(0, 8), in0=lg.ap(0, 8),
                    in1=lg.ap(8, 8), op=ALU.mult),
                    reads=[(rname(lg), "m1mask"), rname(lg)], writes=[(rname(lg), "e2")])
            self.ev("v", lambda E: E.reduce_max(out=st2[:, 282:283], in_=lg.ap(0, 8),
                    axis=AX.X), reads=[(rname(lg), "e2")], writes=[("stat2", "rm2")])
            self.ev("v", lambda E: E.tensor_scalar(out=lg.ap(0, 8), in0=lg.ap(8, 8),
                    scalar1=st2[:, 282:283], scalar2=None, op0=ALU.is_ge),
                    reads=[rname(lg), ("stat2", "rm2")], writes=[(rname(lg), "ind")])
            self.ev("v", lambda E: E.tensor_tensor(out=lg.ap(0, 8), in0=lg.ap(0, 8),
                    in1=lg.ap(8, 8), op=ALU.mult),
                    reads=[(rname(lg), "ind"), rname(lg)], writes=[(rname(lg), "wsel")])
            self.ev("v", lambda E: E.tensor_reduce(out=st2[:, 283:284], in_=lg.ap(0, 8),
                    axis=AX.X, op=ALU.add), reads=[(rname(lg), "wsel")], writes=[("stat2", "wsum")])
            self.ev("v", lambda E: E.reciprocal(out=st2[:, 284:285], in_=st2[:, 283:284]),
                    reads=[("stat2", "wsum")], writes=[("stat2", "winv")])
            self.ev("v", lambda E, ti=ti: E.tensor_scalar(out=wn.ap(ti * 8, 8), in0=lg.ap(0, 8),
                    scalar1=st2[:, 284:285], scalar2=None, op0=ALU.mult),
                    reads=[(rname(lg), "wsel"), ("stat2", "winv")], writes=[(rname(wn), ti)])
        af.free("lg")
        ab.free("xloT")
        PT = ab.alloc("PT", NPF * 2048)
        dwn = ab.alloc("dwn", NPF * D)
        gup = ab.alloc("gup", 4 * 1024)
        glb = ab.alloc("glb", 2048)
        for e in range(10):
            shared = e >= 8
            if shared:
                g_img = _img_kpn(np.asarray(w["L1_ffn_sh_gate"][e - 8], np.float32))
                u_img = _img_kpn(np.asarray(w["L1_ffn_sh_up"][e - 8], np.float32))
                d_img = _img_kpn(np.asarray(w["L1_ffn_sh_down"][e - 8], np.float32), kc=NPF)
            else:
                g_img = _img_kpn(np.asarray(w["L1_ffn_exp_gate"][e], np.float32))
                u_img = _img_kpn(np.asarray(w["L1_ffn_exp_up"][e], np.float32))
                d_img = _img_kpn(np.asarray(w["L1_ffn_exp_down"][e], np.float32), kc=NPF)
            gi = self.inp(f"c_moe_g{e}", g_img)
            ui = self.inp(f"c_moe_u{e}", u_img)
            self.load_reg(dwn, d_img)
            for m in range(NPF):
                slot = (m % 2) * 2
                P.dma("p", gup.ap(slot * 1024, 1024),
                      bass.AP(gi.tensor, m * 128, [[KC * F_, 128], [F_, KC], [1, 128]]),
                      writes=[(rname(gup), ("g", m % 2))])
                P.dma("p", gup.ap(slot * 1024 + 1024, 1024),
                      bass.AP(ui.tensor, m * 128, [[KC * F_, 128], [F_, KC], [1, 128]]),
                      writes=[(rname(gup), ("u", m % 2))])
                for nb in range(4):
                    pairs = [(gup.ap(slot * 1024 + k * 128, 128),
                              xmT.ap(k * 2048 + nb * 512, 512)) for k in range(KC)]
                    self.mm(self.ps[:, nb * 512:(nb + 1) * 512], [nb], pairs,
                            extra_reads=[(rname(gup), ("g", m % 2))] + xmT_reads)
                    pairs = [(gup.ap(slot * 1024 + 1024 + k * 128, 128),
                              xmT.ap(k * 2048 + nb * 512, 512)) for k in range(KC)]
                    self.mm(self.ps[:, (4 + nb) * 512:(5 + nb) * 512], [4 + nb], pairs,
                            extra_reads=[(rname(gup), ("u", m % 2))] + xmT_reads)
                for nb in range(4):
                    self.ev("s", lambda E, nb=nb: E.activation(
                        out=glb.ap(nb * 512, 512), in_=self.ps[:, nb * 512:(nb + 1) * 512],
                        func=AF.Silu),
                        reads=[("PS", nb)], writes=[(rname(glb), nb)])
                    self.ev("v", lambda E, m=m, nb=nb: E.tensor_tensor(
                        out=PT.ap(m * 2048 + nb * 512, 512),
                        in0=self.ps[:, (4 + nb) * 512:(5 + nb) * 512],
                        in1=glb.ap(nb * 512, 512), op=ALU.mult),
                        reads=[("PS", 4 + nb), (rname(glb), nb)],
                        writes=[(rname(PT), m)])
            for tb in range(4):
                for mt in range(4):
                    ti = tb * 4 + mt
                    bks = [(mt % 2) * 2, (mt % 2) * 2 + 1]

                    def d_build(E, tb=tb, mt=mt, bks=bks):
                        inst = None
                        for n2 in range(2):
                            for k in range(NPF):
                                inst = E.matmul(
                                    self.ps[:, (bks[0] + n2) * 512:(bks[0] + n2 + 1) * 512],
                                    ab.t[:, PT.off + k * 2048 + (tb * 4 + mt) * 128:
                                         PT.off + k * 2048 + (tb * 4 + mt + 1) * 128],
                                    ab.t[:, dwn.off + k * D + n2 * 512:dwn.off + k * D + n2 * 512 + 512],
                                    start=(k == 0), stop=(k == NPF - 1))
                        return inst
                    self.mm_raw(d_build, banks=bks,
                                extra_reads=[(rname(PT), k) for k in range(NPF)] + [rname(dwn)])
                    if shared:
                        self.ev("v", lambda E, ti=ti, bks=bks: E.tensor_tensor(
                            out=self.x[:, ti * D:(ti + 1) * D],
                            in0=self.ps[:, bks[0] * 512:(bks[1] + 1) * 512],
                            in1=self.x[:, ti * D:(ti + 1) * D], op=ALU.add),
                            reads=[("PS", bks[0]), ("PS", bks[1]), ("x", ti)],
                            writes=[("x", ti)])
                    else:
                        self.ev("v", lambda E, ti=ti, bks=bks, e=e: E.scalar_tensor_tensor(
                            out=self.x[:, ti * D:(ti + 1) * D],
                            in0=self.ps[:, bks[0] * 512:(bks[1] + 1) * 512],
                            scalar=af.t[:, wn.off + ti * 8 + e:wn.off + ti * 8 + e + 1],
                            in1=self.x[:, ti * D:(ti + 1) * D], op0=ALU.mult, op1=ALU.add),
                            reads=[("PS", bks[0]), ("PS", bks[1]), ("x", ti), (rname(wn), ti)],
                            writes=[("x", ti)])
        ab.free("PT")
        ab.free("dwn")
        ab.free("gup")
        ab.free("glb")
        ab.free("xmT")
        af.free("wn")
        self.ln_x_inplace()


MK.l1_ffn = MoEMixin.l1_ffn
